# version 17
# baseline (speedup 1.0000x reference)
import numpy as np
import ml_dtypes
import concourse.bass as bass
import concourse.mybir as mybir
from concourse.bass_core_rust import IndirectOffsetOnAxis
from concourse.bass_utils import run_bass_kernel_spmd

F32 = mybir.dt.float32
BF16 = mybir.dt.bfloat16
I32 = mybir.dt.int32
AF = mybir.ActivationFunctionType
ALU = mybir.AluOpType
AX = mybir.AxisListType

T, NT, D, FC = 2048, 16, 1024, 8
H, HD, G4 = 16, 64, 4
NG, CAP, E = 16384, 4096, 8
KB = 1024
BASE = 20 * KB
OA, OB, OC, OD, OE, OF = (BASE + 0, BASE + 32 * KB, BASE + 64 * KB,
                          BASE + 96 * KB, BASE + 128 * KB, BASE + 160 * KB)
OG = OF
OH = BASE + 192 * KB


class Sem:
    def __init__(self, nc, ctx, name):
        self.h = ctx.enter_context(nc.semaphore(name))
        self.n = 0


class Build:
    def __init__(self, nc, ctx, dbg=()):
        self.nc, self.ctx, self.dbg = nc, ctx, set(dbg)
        self.q = {k: [] for k in ["sync", "tensor", "vector", "scalar", "gpsimd"]}
        self.s = {k: Sem(nc, ctx, "s" + k) for k in
                  ["LD", "GD", "CC", "TE", "VE", "SE", "GE", "ST"]}

    def wait(self, eng, sem, val):
        h = self.s[sem].h
        self.q[eng].append(lambda e, h=h, v=val: e.wait_ge(h, v))

    def _do(self, eng, semname, fn, inc, drain=True):
        s = self.s[semname]
        if inc:
            s.n += 1
        def run(e, fn=fn, inc=inc, h=s.h, drain=drain):
            ins = fn(e)
            if drain:
                ins = e.drain()
            if inc:
                ins.then_inc(h, 1)
        self.q[eng].append(run)
        return s.n

    def vdo(self, fn, inc=False, drain=True):
        return self._do("vector", "VE", fn, inc, drain)

    def sdo(self, fn, inc=False, drain=True):
        return self._do("scalar", "SE", fn, inc, drain)

    def gdo(self, fn, inc=False, drain=True):
        return self._do("gpsimd", "GE", fn, inc, drain)

    def tdo(self, fn, inc=True):
        s = self.s["TE"]
        if inc:
            s.n += 1
        def run(t, fn=fn, inc=inc, h=s.h):
            ins = fn(t)
            if inc:
                ins.then_inc(h, 1)
        self.q["tensor"].append(run)
        return s.n

    def dma(self, eng, out, in_, sem=None, **kw):
        sem = sem or ("GD" if eng == "gpsimd" else "LD")
        s = self.s[sem]
        prev = s.n
        s.n += 16
        def run(e, out=out, in_=in_, h=s.h, kw=kw, prev=prev):
            if prev:
                e.wait_ge(h, prev)
            e.dma_start(out=out, in_=in_, **kw).then_inc(h, 16)
        self.q[eng].append(run)
        return s.n

    def idma(self, out, out_off, in_, in_off):
        s = self.s["GD"]
        prev = s.n
        s.n += 16
        def run(g, out=out, oo=out_off, in_=in_, io=in_off, h=s.h, prev=prev):
            if prev:
                g.wait_ge(h, prev)
            g.indirect_dma_start(out, oo, in_, io).then_inc(h, 16)
        self.q["gpsimd"].append(run)
        return s.n

    def coll(self, ins_ap, outs_ap):
        s = self.s["CC"]
        s.n += 1
        def run(g, i=ins_ap, o=outs_ap, h=s.h):
            g.collective_compute("AllGather", ALU.bypass,
                                 replica_groups=[list(range(8))],
                                 ins=[i], outs=[o]).then_inc(h, 1)
        self.q["gpsimd"].append(run)
        return s.n


def build_nc(dbg=()):
    from contextlib import ExitStack
    nc = bass.Bass("TRN2", target_bir_lowering=False)
    ctx = ExitStack()
    b = Build(nc, ctx, dbg)
    dt = nc.dram_tensor

    x_in = dt("x_in", [T, D], F32, kind="ExternalInput").ap()
    out_o = dt("out_o", [T, D], F32, kind="ExternalOutput").ap()
    wq_in = dt("wq_in", [D, D], F32, kind="ExternalInput").ap()
    wkv_in = dt("wkv_in", [D, 512], F32, kind="ExternalInput").ap()
    wout_in = dt("wout_in", [D, D], F32, kind="ExternalInput").ap()
    wqkv_in = dt("wqkv_in", [D, 3072], F32, kind="ExternalInput").ap()
    wproj_in = dt("wproj_in", [D, D], F32, kind="ExternalInput").ap()
    wgate_in = dt("wgate_in", [D, E], F32, kind="ExternalInput").ap()
    w1_in = dt("w1_in", [D, 4096], F32, kind="ExternalInput").ap()
    w2_in = dt("w2_in", [4096, D], F32, kind="ExternalInput").ap()
    lnA_g = dt("lnA_g", [128, D], F32, kind="ExternalInput").ap()
    lnA_b = dt("lnA_b", [128, D], F32, kind="ExternalInput").ap()
    ln1_g = dt("ln1_g", [128, D], F32, kind="ExternalInput").ap()
    ln1_b = dt("ln1_b", [128, D], F32, kind="ExternalInput").ap()
    ln2_g = dt("ln2_g", [128, D], F32, kind="ExternalInput").ap()
    ln2_b = dt("ln2_b", [128, D], F32, kind="ExternalInput").ap()
    qn_s = dt("qn_s", [128, D], F32, kind="ExternalInput").ap()
    qn_b = dt("qn_b", [128, D], F32, kind="ExternalInput").ap()
    kn_s = dt("kn_s", [128, 256], F32, kind="ExternalInput").ap()
    kn_b = dt("kn_b", [128, 256], F32, kind="ExternalInput").ap()
    b1_in = dt("b1_in", [128, 32], F32, kind="ExternalInput").ap()
    b2_in = dt("b2_in", [128, D], F32, kind="ExternalInput").ap()
    ident_in = dt("ident_in", [128, 128], BF16, kind="ExternalInput").ap()
    ltri_in = dt("ltri_in", [128, 128], BF16, kind="ExternalInput").ap()
    ones_in = dt("ones_in", [128, 128], BF16, kind="ExternalInput").ap()
    cmask_in = dt("cmask_in", [128, 128], BF16, kind="ExternalInput").ap()
    rCq_in = dt("rCq_in", [128, NT, 32], BF16, kind="ExternalInput").ap()
    rSq_in = dt("rSq_in", [128, NT, 32], BF16, kind="ExternalInput").ap()
    rCk_in = dt("rCk_in", [128, NT, 32], BF16, kind="ExternalInput").ap()
    rSk_in = dt("rSk_in", [128, NT, 32], BF16, kind="ExternalInput").ap()
    iota8_in = dt("iota8_in", [128, 8], F32, kind="ExternalInput").ap()
    esel_in = dt("esel_in", [128, 8], F32, kind="ExternalInput").ap()
    ownB_in = dt("ownB_in", [128, 8], F32, kind="ExternalInput").ap()
    iotarow_in = dt("iotarow_in", [128, 128], F32, kind="ExternalInput").ap()
    iotaslot_in = dt("iotaslot_in", [128, 32], F32, kind="ExternalInput").ap()
    pcol_in = dt("pcol_in", [128, 1], BF16, kind="ExternalInput").ap()

    d_x1 = dt("d_x1", [T, D], F32).ap()
    d_y = dt("d_y", [T, D], F32).ap()
    d_x2 = dt("d_x2", [T, D], F32).ap()
    d_h2 = dt("d_h2", [T, D], BF16).ap()
    d_ep1 = dt("d_ep1", [16, 128], BF16).ap()
    d_ep2 = dt("d_ep2", [16, 128], BF16).ap()
    ag_e1 = dt("ag_e1", [8, 16, 128], BF16).ap()
    ag_e2 = dt("ag_e2", [8, 16, 128], BF16).ap()
    ag_h2 = dt("ag_h2", [4, 8, 512, D], BF16).ap()
    d_prow = dt("d_prow", [128, 128], F32).ap()
    d_exp = dt("d_exp", [CAP, D], BF16).ap()
    ag_exp = dt("ag_exp", [4, E, 1024, D], BF16).ap()

    with ctx:
        A = nc.alloc_sbuf_tensor_at
        hf = A("hf", [128, FC, T], BF16, offset=OA)
        q1f = A("q1f", [128, FC, T], BF16, offset=OB)
        wq_sb = A("wq_sb", [128, FC, D], BF16, offset=OC)
        wkv_sb = A("wkv_sb", [128, FC, 512], BF16, offset=OC + 16 * KB)
        wqkvh0 = A("wqkvh0", [128, FC, 192], BF16, offset=OC)
        wqkvh1 = A("wqkvh1", [128, FC, 192], BF16, offset=OC + 3 * KB)
        qkvt = A("qkvt", [128, NT, 192], BF16, offset=OC + 8 * KB)
        q2r = A("q2r", [128, NT, HD], BF16, offset=OC + 14 * KB)
        k2r = A("k2r", [128, NT, HD], BF16, offset=OC + 16 * KB)
        q2f = A("q2f", [64, T], BF16, offset=OC + 18 * KB)
        k2f = A("k2f", [64, T], BF16, offset=OC + 22 * KB)
        rs1 = A("rs1", [128, NT, 32], BF16, offset=OC + 26 * KB)
        rs2 = A("rs2", [128, NT, 32], BF16, offset=OC + 27 * KB)
        rs3 = A("rs3", [128, NT, 32], BF16, offset=OC + 28 * KB)
        rs4 = A("rs4", [128, NT, 32], BF16, offset=OC + 29 * KB)
        wproj_sb = A("wproj_sb", [128, FC, D], BF16, offset=OC + 6 * KB)
        k1f = A("k1f", [128, 2, T], BF16, offset=OD)
        v1t = A("v1t", [128, NT, 256], BF16, offset=OD + 8 * KB)
        wout_sb = A("wout_sb", [128, FC, D], BF16, offset=OD + 16 * KB)
        e_xt = A("e_xt", [128, D], F32, offset=OE)
        e_c1 = A("e_c1", [128, D], F32, offset=OE + 4 * KB)
        e_c2 = A("e_c2", [128, D], F32, offset=OE + 8 * KB)
        e_c3 = A("e_c3", [128, D], F32, offset=OE + 12 * KB)
        e_b1 = A("e_b1", [128, D], BF16, offset=OE + 16 * KB)
        e_b2 = A("e_b2", [128, D], BF16, offset=OE + 18 * KB)
        e_s4 = A("e_s4", [128, NT, 8], F32, offset=OE + 20 * KB)
        bn6 = A("bn6", [128, 2, 6], F32, offset=OH + 10 * KB + 512)
        bn2 = A("bn2", [128, 2], F32, offset=OH + 10 * KB + 576)
        e_s5 = A("e_s5", [128, D], F32, offset=OF + 24 * KB)
        oall = A("oall", [128, NT, D], BF16, offset=OF)
        h2f = A("h2f", [128, FC, T], BF16, offset=OB + 16 * KB)
        g_v0 = A("g_v0", [128, D], F32, offset=OG)
        g_v1 = A("g_v1", [128, D], F32, offset=OG + 4 * KB)
        g_kn0 = A("g_kn0", [128, 256], F32, offset=OF + 8 * KB)
        g_kn1 = A("g_kn1", [128, 256], F32, offset=OF + 9 * KB)
        PT2 = A("PT2", [128, NT, 128], BF16, offset=OE + 25 * KB)
        b2sb = A("b2sb", [128, D], F32, offset=OE + 12 * KB)
        ident = A("ident", [128, 128], BF16, offset=OH)
        ltri = A("ltri", [128, 128], BF16, offset=OH + 256)
        ones = A("ones", [128, 128], BF16, offset=OH + 512)
        cmask = A("cmask", [128, 128], BF16, offset=OH + 768)
        rCq = A("rCq", [128, NT, 32], BF16, offset=OH + 1 * KB)
        rSq = A("rSq", [128, NT, 32], BF16, offset=OH + 2 * KB)
        rCk = A("rCk", [128, NT, 32], BF16, offset=OH + 3 * KB)
        rSk = A("rSk", [128, NT, 32], BF16, offset=OH + 4 * KB)
        iota8 = A("iota8", [128, 8], F32, offset=OH + 5 * KB)
        esel = A("esel", [128, 8], F32, offset=OH + 5 * KB + 32)
        ownB = A("ownB", [128, 8], F32, offset=OH + 5 * KB + 64)
        epsv = A("epsv", [128, 2], F32, offset=OH + 5 * KB + 96)
        iotarow = A("iotarow", [128, 128], F32, offset=OH + 5 * KB + 512)
        iotaslot = A("iotaslot", [128, 32], F32, offset=OH + 6 * KB)
        pcol = A("pcol", [128, 1], BF16, offset=OH + 6 * KB + 128)
        b1sb = A("b1sb", [128, 32], F32, offset=OH + 6 * KB + 256)
        wg_sb = A("wg_sb", [128, FC, 8], BF16, offset=OH + 6 * KB + 384)
        stat0 = A("stat0", [128, 16], F32, offset=OH + 6 * KB + 512)
        stat1 = A("stat1", [128, 16], F32, offset=OH + 6 * KB + 576)
        stat2 = A("stat2", [128, 16], F32, offset=OH + 6 * KB + 640)
        stat3 = A("stat3", [128, 16], F32, offset=OH + 6 * KB + 704)
        negmax = A("negmax", [128, 1], F32, offset=OH + 6 * KB + 768)
        sums = A("sums", [128, 1], F32, offset=OH + 6 * KB + 800)
        recs = A("recs", [128, 1], F32, offset=OH + 6 * KB + 832)
        lgt = A("lgt", [128, NT, 8], F32, offset=OH + 7 * KB)
        gts = A("gts", [128, NT, 8], F32, offset=OH + 7 * KB + 512)
        t2a = A("t2a", [128, NT, 8], F32, offset=OH + 8 * KB)
        t2b = A("t2b", [128, NT, 8], F32, offset=OH + 8 * KB + 512)
        e1t = A("e1t", [128, NT], F32, offset=OH + 9 * KB)
        e2t = A("e2t", [128, NT], F32, offset=OH + 9 * KB + 64)
        g1t = A("g1t", [128, NT], F32, offset=OH + 9 * KB + 128)
        g2t = A("g2t", [128, NT], F32, offset=OH + 9 * KB + 192)
        pos1 = A("pos1", [128, NT], F32, offset=OH + 9 * KB + 256)
        pos2 = A("pos2", [128, NT], F32, offset=OH + 9 * KB + 320)
        sc1 = A("sc1", [128, NT], F32, offset=OH + 9 * KB + 384)
        sc2 = A("sc2", [128, NT], F32, offset=OH + 9 * KB + 448)
        idx1 = A("idx1", [128, NT], I32, offset=OH + 9 * KB + 512)
        idx2 = A("idx2", [128, NT], I32, offset=OH + 9 * KB + 576)
        tokl = A("tokl", [128, 32], I32, offset=OH + 9 * KB + 640)
        prowg = A("prowg", [128, 32], F32, offset=OH + 9 * KB + 768)
        fidx = A("fidx", [128, 32], I32, offset=OH + 9 * KB + 896)
        kp1 = A("kp1", [128, NT], F32, offset=OH + 10 * KB)
        kp2 = A("kp2", [128, NT], F32, offset=OH + 10 * KB + 64)
        P_sb = A("P_sb", [128, T], BF16, offset=OE + 21 * KB)
        P_sb2 = A("P_sb2", [128, T], BF16, offset=OE)
        qkvtB = A("qkvtB", [128, NT, 192], BF16, offset=OB)
        q2fB = A("q2fB", [64, T], BF16, offset=OB + 6 * KB)
        k2fB = A("k2fB", [64, T], BF16, offset=OB + 10 * KB)
        rtmp = A("rtmp", [128, 32, 32], BF16, offset=OB + 14 * KB)
        PT2b = A("PT2b", [128, NT, 128], BF16, offset=OE + 4 * KB)
        negmax2 = A("negmax2", [128, 1], F32, offset=OH + 6 * KB + 864)
        sums2 = A("sums2", [128, 1], F32, offset=OH + 6 * KB + 896)
        recs2 = A("recs2", [128, 1], F32, offset=OH + 6 * KB + 928)
        smp4a = A("smp4a", [128, 4], F32, offset=OH + 6 * KB + 960)
        sumsA = A("sumsA", [128, NT, 16], F32, offset=OC)
        rsA = A("rsA", [128, NT, 16], F32, offset=OC + 1 * KB)
        g45a = A("g45a", [128, D], F32, offset=OC + 2 * KB)
        g45b = A("g45b", [128, D], F32, offset=OC + 6 * KB)
        s45 = A("s45", [128, D], F32, offset=OC + 10 * KB)
        sumsB = A("sumsB", [128, NT, 16], F32, offset=OE + 9 * KB)
        rsB = A("rsB", [128, NT, 16], F32, offset=OE + 11 * KB)
        smp4b = A("smp4b", [128, 4], F32, offset=OH + 6 * KB + 992)
        e1g = A("e1g", [128, 128], BF16, offset=OE)
        e2g = A("e2g", [128, 128], BF16, offset=OE + 256)
        mgs1 = A("mgs1", [128, 128], F32, offset=OE + 512)
        maskf = A("maskf", [128, E, 128], F32, offset=OE + 1 * KB)
        maskb = A("maskb", [128, E, 128], BF16, offset=OE + 5 * KB)
        mgs2 = A("mgs2", [128, 128], F32, offset=OE + 7 * KB + 512)
        ppref = A("ppref", [128, E, 128], F32, offset=OE + 8 * KB)
        colsum = A("colsum", [128, E, 128], F32, offset=OE + 12 * KB)
        csh1 = A("csh1", [128, E, 128], F32, offset=OE + 16 * KB)
        csh2 = A("csh2", [128, E, 128], F32, offset=OE + 20 * KB)
        posg = A("posg", [128, E, 128], F32, offset=OE + 24 * KB)
        ownps = A("ownps", [128, E, NT], F32, offset=OE + 28 * KB)
        myppref = A("myppref", [128, 128], F32, offset=OE + 28 * KB + 512)
        mycolpre = A("mycolpre", [128, 128], F32, offset=OE + 29 * KB)
        mymask = A("mymask", [128, 128], F32, offset=OE + 29 * KB + 512)
        ohf = A("ohf", [128, 128], F32, offset=OE + 30 * KB)
        ohbf = A("ohbf", [128, 128], BF16, offset=OE + 30 * KB + 512)
        prow_sb = A("prow_sb", [128, 128], F32, offset=OE + 31 * KB)
        cmpb = A("cmpb", [128, 32, 128], F32, offset=OF)
        srA = A("srA", [128, 32, 128], F32, offset=OF + 16 * KB)
        colS = A("colS", [128, 32], F32, offset=OH + 10 * KB + 128)
        cbS = A("cbS", [128, 32], F32, offset=OH + 10 * KB + 256)
        rkS = A("rkS", [128, 32], F32, offset=OH + 10 * KB + 384)
        w2sb = A("w2sb", [128, 32, D], BF16, offset=OA)
        w1sb = A("w1sb", [128, FC, 4096], BF16, offset=OC)
        xg = A("xg", [128, D], BF16, offset=OE)
        xf = A("xf", [128, FC, 128], BF16, offset=OE + 2 * KB)
        hmb = A("hmb", [128, FC, 128], BF16, offset=OE + 4 * KB)
        hsc = A("hsc", [128, D], F32, offset=OE + 6 * KB)
        outt = A("outt", [128, D], BF16, offset=OE + 10 * KB)
        xg2 = A("xg2", [128, D], BF16, offset=OE + 25 * KB)
        hsc2 = A("hsc2", [128, D], F32, offset=OE + 29 * KB)
        hmb2 = A("hmb2", [128, FC, 128], BF16, offset=OF + 20 * KB)
        xf2 = A("xf2", [128, FC, 128], BF16, offset=OE + 27 * KB)
        cA1 = A("cA1", [128, D], BF16, offset=OE + 16 * KB)
        cA2 = A("cA2", [128, D], BF16, offset=OE + 18 * KB)
        ct1 = A("ct1", [128, D], F32, offset=OF)
        ct2 = A("ct2", [128, D], F32, offset=OF + 4 * KB)
        cx1 = A("cx1", [128, D], F32, offset=OF + 8 * KB)
        cx2 = A("cx2", [128, D], F32, offset=OF + 12 * KB)
        co = A("co", [128, D], F32, offset=OF + 16 * KB)

        PS = nc.alloc_psum_tensor("PS", [128, 4096], F32)
        psS = PS[:, 0:2048]
        psSr = [PS[:, 0:2048], PS[:, 2048:4096]]
        psPTr = [PS[:, 0:2048].bitcast(BF16), PS[:, 2048:4096].bitcast(BF16)]
        psOr = [PS[:, 384:448], PS[:, 2432:2496]]
        psXa = PS[:, 2048:2240]
        psXb = PS[:, 2560:2752]
        psXp = [PS[:, 768:960], PS[:, 2816:3008]]
        psTq = [PS[0:64, 1280:1344].bitcast(BF16), PS[0:64, 3328:3392].bitcast(BF16)]
        psTk = [PS[0:64, 1344:1408].bitcast(BF16), PS[0:64, 3392:3456].bitcast(BF16)]
        psX = PS[:, 2048:3072]
        psPT = PS[:, 2048:2560].bitcast(BF16)
        psO = PS[:, 3072:3136]
        psL = PS[:, 3136:3144]
        psT = PS[:, 3584:4096].bitcast(BF16)
        psPP = PS[:, 0:1024]
        psCS = PS[:, 1024:2048]
        psPR = PS[:, 2560:3072]
        psHm = PS[:, 0:1024]
        psOm = PS[:, 1024:2048]

        S = b.s
        LD, GD, CC, TE, VE, SE, GE, ST = (S[k] for k in
                                          ["LD", "GD", "CC", "TE", "VE", "SE", "GE", "ST"])

        def dbg_dump(name, src, shape, dtyp, waits):
            if name not in b.dbg:
                return
            o = dt("dbg_" + name, shape, dtyp, kind="ExternalOutput").ap()
            for sem, val in waits:
                b.wait("sync", sem, val)
            b.dma("sync", o, src, sem="ST")

        for dst_t, src in [(ident, ident_in), (ltri, ltri_in), (ones, ones_in),
                           (cmask, cmask_in), (rCq, rCq_in), (rSq, rSq_in),
                           (rCk, rCk_in), (rSk, rSk_in), (iota8, iota8_in),
                           (esel, esel_in), (ownB, ownB_in), (iotarow, iotarow_in),
                           (iotaslot, iotaslot_in), (pcol, pcol_in), (b1sb, b1_in)]:
            b.dma("sync", dst_t[:], src)
        b.dma("sync", g_v0[:], lnA_g)
        b.dma("sync", g_v1[:], lnA_b)
        ld0 = LD.n
        b.wait("vector", "LD", ld0)
        b.vdo(lambda v: v.tensor_scalar(epsv[:, 0:1], ones[:, 0:1], 0.0, 1e-5, ALU.mult, ALU.add))
        b.vdo(lambda v: v.tensor_scalar(epsv[:, 1:2], ones[:, 0:1], 0.0, 1e-6, ALU.mult, ALU.add), inc=True)
        b.dma("gpsimd", wq_sb[:], wq_in.rearrange("(j p) o -> p j o", p=128))
        b.dma("gpsimd", wkv_sb[:], wkv_in.rearrange("(j p) o -> p j o", p=128))
        b.dma("gpsimd", wg_sb[:], wgate_in.rearrange("(j p) o -> p j o", p=128))
        gd0 = GD.n

        def ln_plain_tile(src_sb, dst_sb, eps, sA, sB, tmp):
            ev = epsv[:, 0:1] if eps == 1e-5 else epsv[:, 1:2]
            b.vdo(lambda v: (v.bn_stats(bn6[:, 0, :], src_sb[:, 0:512]),
                             v.bn_stats(bn6[:, 1, :], src_sb[:, 512:1024]))[-1], drain=True)
            b.vdo(lambda v: v.bn_aggr(bn2[:], bn6[:]), inc=True, drain=True)
            b.wait("scalar", "VE", b.s["VE"].n)
            sv = b.sdo(lambda s: s.activation(out=sB[:, 2:3], in_=bn2[:, 1:2], func=AF.Sqrt,
                                              bias=ev, scale=1.0), inc=True)
            b.wait("vector", "SE", sv)
            b.vdo(lambda v: v.reciprocal(sA[:, 2:3], sB[:, 2:3]), drain=True)
            b.vdo(lambda v: v.scalar_tensor_tensor(dst_sb[:], src_sb[:], bn2[:, 0:1], 
                                                   sA[:, 2:3].broadcast_to([128, D]),
                                                   ALU.subtract, ALU.mult), drain=False)

        def ln_gb_tile(src_sb, dst_sb, eps, gm, bt, tmp, s5=None):
            ev = epsv[:, 0:1] if eps == 1e-5 else epsv[:, 1:2]
            b.vdo(lambda v: (v.bn_stats(bn6[:, 0, :], src_sb[:, 0:512]),
                             v.bn_stats(bn6[:, 1, :], src_sb[:, 512:1024]))[-1], drain=True)
            b.vdo(lambda v: v.bn_aggr(bn2[:], bn6[:]), inc=True, drain=True)
            b.wait("scalar", "VE", b.s["VE"].n)
            sv = b.sdo(lambda s: s.activation(out=stat1[:, 6:7], in_=bn2[:, 1:2], func=AF.Sqrt,
                                              bias=ev, scale=1.0), inc=True)
            b.wait("vector", "SE", sv)
            b.vdo(lambda v: v.reciprocal(stat0[:, 6:7], stat1[:, 6:7]), drain=True)
            b.vdo(lambda v: v.scalar_tensor_tensor(tmp[:], src_sb[:], bn2[:, 0:1],
                                                   stat0[:, 6:7].broadcast_to([128, D]),
                                                   ALU.subtract, ALU.mult), drain=True)
            s5 = e_s5 if s5 is None else s5
            b.vdo(lambda v: v.tensor_mul(s5[:], tmp[:], gm[:]), drain=True)
            b.vdo(lambda v: v.tensor_add(dst_sb[:], s5[:], bt[:]), drain=False)

        def to_featmajor(i, src_bf, dst_ff, nch=FC):
            def tp(t, i=i):
                for j in range(nch):
                    ins = t.transpose(psT[:, 128 * j:128 * (j + 1)],
                                      src_bf[:, 128 * j:128 * (j + 1)], ident[:])
                return ins
            tv = b.tdo(tp)
            b.wait("scalar", "TE", tv)
            se = b.sdo(lambda s, i=i: s.copy(dst_ff[:, :, 128 * i:128 * (i + 1)],
                                             psT[:, 0:128 * nch].rearrange("p (c q) -> p c q", c=nch)),
                       inc=True, drain=False)
            b.wait("tensor", "SE", se)
            b.wait("vector", "SE", se)
            return se

        b.wait("vector", "LD", ld0)
        for i in range(NT):
            if i > 0:
                b.wait("sync", "VE", VE.n)
            b.dma("sync", e_xt[:], x_in[128 * i:128 * (i + 1), :])
            b.wait("vector", "LD", LD.n)
            ln_plain_tile(e_xt, e_c2, 1e-5, stat0, stat1, e_c1)
            ln_gb_tile(e_c2, e_c3, 1e-6, g_v0, g_v1, e_c1)
            b.vdo(lambda v: v.tensor_copy(e_b1[:], e_c3[:]), inc=True)
            b.wait("tensor", "VE", VE.n)
            to_featmajor(i, e_b1, hf)
        ph1_se = SE.n
        dbg_dump("xnf", hf[:], [128, FC, T], BF16, [("SE", ph1_se)])

        b.wait("tensor", "GD", gd0)
        b.wait("sync", "SE", ph1_se)
        b.dma("sync", e_c2[:], qn_s)
        b.dma("sync", e_c3[:], qn_b)
        b.wait("vector", "LD", LD.n)
        b.vdo(lambda v: v.tensor_scalar_mul(g_v0[:], e_c2[:], 0.125))
        b.vdo(lambda v: v.tensor_scalar_mul(g_v1[:], e_c3[:], 0.125), inc=True)
        vqn = VE.n
        b.wait("sync", "VE", vqn)
        b.dma("sync", g_kn0[:], kn_s)
        b.dma("sync", g_kn1[:], kn_b)
        ld_kn = LD.n
        b.wait("vector", "LD", ld_kn)

        def headln(tile_f32, n_grp, gm, bt, eps):
            w = n_grp * HD
            src3 = tile_f32[:, 0:w].rearrange("p (a c) -> p a c", a=n_grp)
            b.vdo(lambda v: v.reduce_sum(stat0[:, 0:n_grp], src3, axis=AX.X))
            b.vdo(lambda v: v.tensor_scalar_mul(stat1[:, 0:n_grp], stat0[:, 0:n_grp], 1.0 / HD))
            b.vdo(lambda v: v.tensor_sub(e_c1[:, 0:w].rearrange("p (a c) -> p a c", a=n_grp), src3,
                                         stat1[:, 0:n_grp].unsqueeze(2).broadcast_to([128, n_grp, HD])))
            b.vdo(lambda v: v.tensor_mul(e_c2[:, 0:w], e_c1[:, 0:w], e_c1[:, 0:w]))
            ev = epsv[:, 0:1] if eps == 1e-5 else epsv[:, 1:2]
            b.vdo(lambda v: v.reduce_sum(stat2[:, 0:n_grp],
                                         e_c2[:, 0:w].rearrange("p (a c) -> p a c", a=n_grp),
                                         axis=AX.X), inc=True, drain=False)
            b.wait("scalar", "VE", b.s["VE"].n)
            sv = b.sdo(lambda s: s.activation(out=stat3[:, 0:n_grp], in_=stat2[:, 0:n_grp],
                                              func=AF.Sqrt, bias=ev, scale=1.0 / HD), inc=True)
            b.wait("vector", "SE", sv)
            b.vdo(lambda v: v.reciprocal(stat0[:, 0:n_grp], stat3[:, 0:n_grp]))
            b.vdo(lambda v: v.tensor_mul(e_c2[:, 0:w].rearrange("p (a c) -> p a c", a=n_grp),
                                         e_c1[:, 0:w].rearrange("p (a c) -> p a c", a=n_grp),
                                         stat0[:, 0:n_grp].unsqueeze(2).broadcast_to([128, n_grp, HD])))
            b.vdo(lambda v: v.tensor_mul(e_c1[:, 0:w], e_c2[:, 0:w], gm))
            b.vdo(lambda v: v.tensor_add(e_c3[:, 0:w], e_c1[:, 0:w], bt))

        for i in range(NT):
            def qmm(t, i=i):
                for j in range(FC):
                    for h2 in range(2):
                        ins = t.matmul(psX[:, 512 * h2:512 * (h2 + 1)],
                                       hf[:, j, 128 * i:128 * (i + 1)],
                                       wq_sb[:, j, 512 * h2:512 * (h2 + 1)],
                                       start=(j == 0), stop=(j == FC - 1))
                return ins
            tq = b.tdo(qmm)
            b.wait("vector", "TE", tq)
            b.vdo(lambda v: v.tensor_copy(e_xt[:], psX))
            headln(e_xt, H, g_v0[:], g_v1[:], 1e-6)
            b.vdo(lambda v: v.tensor_copy(e_b1[:], e_c3[:]), inc=True)
            b.wait("tensor", "VE", VE.n)
            to_featmajor(i, e_b1, q1f)
            def kvmm(t, i=i):
                for j in range(FC):
                    ins = t.matmul(psX[:, 0:512], hf[:, j, 128 * i:128 * (i + 1)],
                                   wkv_sb[:, j, :], start=(j == 0), stop=(j == FC - 1))
                return ins
            tk = b.tdo(kvmm)
            b.wait("vector", "TE", tk)
            b.vdo(lambda v: v.tensor_copy(e_xt[:, 0:256], psX[:, 0:256]))
            vv = b.vdo(lambda v, i=i: v.tensor_copy(v1t[:, i, :], psX[:, 256:512]), inc=True)
            headln(e_xt, G4, g_kn0[:], g_kn1[:], 1e-6)
            b.vdo(lambda v: v.tensor_copy(e_b1[:, 0:256], e_c3[:, 0:256]), inc=True)
            b.wait("tensor", "VE", VE.n)
            def ktp(t, i=i):
                for j in range(2):
                    ins = t.transpose(psT[:, 128 * j:128 * (j + 1)],
                                      e_b1[:, 128 * j:128 * (j + 1)], ident[:])
                return ins
            tk2 = b.tdo(ktp)
            b.wait("scalar", "TE", tk2)
            sk = b.sdo(lambda s, i=i: s.copy(k1f[:, :, 128 * i:128 * (i + 1)],
                                             psT[:, 0:256].rearrange("p (c q) -> p c q", c=2)), inc=True)
            b.wait("tensor", "SE", sk)
            b.wait("vector", "SE", sk)
        ph2_se, ph2_ve = SE.n, VE.n
        dbg_dump("q1f", q1f[:], [128, FC, T], BF16, [("SE", ph2_se)])
        dbg_dump("k1f", k1f[:], [128, 2, T], BF16, [("SE", ph2_se)])
        dbg_dump("v1t", v1t[:], [128, NT, 256], BF16, [("VE", ph2_ve)])

        b.wait("gpsimd", "SE", ph2_se)
        b.dma("gpsimd", wout_sb[:], wout_in.rearrange("(j p) o -> p j o", p=128))
        gd_wout = GD.n

        b.wait("tensor", "SE", ph2_se)
        b.wait("tensor", "VE", ph2_ve)

        Pp = [P_sb, P_sb2]
        PTp = [PT2, PT2b]
        nmp = [negmax, negmax2]
        smp = [sums, sums2]
        rcp = [recs, recs2]

        smp4 = [smp4a, smp4b]

        def attn_pipe(iters, masked, inject_t=None, inject_v=None, inject_s=None):
            N = len(iters)
            tS = [0] * N
            tPV = [0] * N
            vMax = [0] * N
            vRc = [0] * N
            vOut = [0] * N
            injV = [0] * N
            injS = [0] * N
            sExp4 = [[0] * 4 for _ in range(N)]

            def groups(nk):
                return [(4 * i, min(4, nk - 4 * i)) for i in range((nk + 3) // 4)]

            def emit_S(n):
                it = iters[n]
                r = n % 2
                nk = it["nk"]
                kw = nk * 128
                if n >= 2:
                    b.wait("tensor", "SE", vOut[n - 2])
                    if injV[n - 2]:
                        b.wait("tensor", "VE", injV[n - 2])
                    if injS[n - 2]:
                        b.wait("tensor", "SE", injS[n - 2])
                sw = it.get("sw")
                if sw is not None:
                    swv = sw()
                    if swv:
                        b.wait("tensor", "SE", swv)
                def smm(t, it=it, r=r, nk=nk, kw=kw, masked=masked):
                    nchunk = (kw + 511) // 512
                    dbank = ((nk - 1) * 128) // 512
                    ins = None
                    for kc in range(nchunk):
                        w = min(512, kw - 512 * kc)
                        ins = t.matmul(psSr[r][:, 512 * kc:512 * kc + w], it["lq"],
                                       it["rk"](512 * kc, w),
                                       start=True, stop=(not masked) or (kc != dbank))
                    if masked:
                        ins = t.matmul(psSr[r][:, 128 * (nk - 1):128 * nk], ident[:],
                                       cmask[:], start=False, stop=True)
                    return ins
                tS[n] = b.tdo(smm)

            def emit_vmax(n):
                r = n % 2
                kw = iters[n]["nk"] * 128
                b.wait("vector", "TE", tS[n])
                vMax[n] = b.vdo(lambda v, r=r, kw=kw: v.reduce_max(
                    nmp[r][:], psSr[r][:, 0:kw].rearrange("p (a c) -> p a c", a=1),
                    axis=AX.X, negate=True), inc=True, drain=False)

            def emit_exp(n):
                r = n % 2
                it = iters[n]
                grs = groups(it["nk"])
                ng = len(grs)
                b.wait("scalar", "VE", vMax[n])
                for gi, (st, sz) in enumerate(grs):
                    acc = it["sacc"] if ng == 1 else smp4[r][:, gi:gi + 1]
                    sExp4[n][gi] = b.sdo(lambda sc, r=r, st=st, sz=sz, acc=acc: sc.activation(
                        out=Pp[r][:, 128 * st:128 * (st + sz)],
                        in_=psSr[r][:, 128 * st:128 * (st + sz)], func=AF.Exp,
                        bias=nmp[r][:], scale=1.0,
                        accum_out=acc), inc=True, drain=False)

            def emit_vsum(m):
                it = iters[m]
                r = m % 2
                ng = len(groups(it["nk"]))
                if ng >= 2:
                    b.wait("vector", "SE", sExp4[m][ng - 1])
                    b.vdo(lambda v, r=r, ng=ng, it=it: v.reduce_sum(
                        it["sacc"], smp4[r][:, 0:ng].rearrange("p (a c) -> p a c", a=1),
                        axis=AX.X), inc=True, drain=False)

            def emit_tail(m):
                it = iters[m]
                r = m % 2
                nk = it["nk"]
                grs = groups(nk)
                ng = len(grs)
                ev4 = []
                for gi, (st, sz) in enumerate(grs):
                    b.wait("tensor", "SE", sExp4[m][gi])
                    def ptp(t, r=r, gi=gi, sz=sz):
                        for i in range(sz):
                            ins = t.transpose(
                                psPTr[r][:, 1024 * gi + 128 * i:1024 * gi + 128 * (i + 1)],
                                Pp[r][:, 128 * (4 * gi + i):128 * (4 * gi + i + 1)], ident[:])
                        return ins
                    tT = b.tdo(ptp)
                    if gi % 2 == 0:
                        b.wait("vector", "TE", tT)
                        c = b.vdo(lambda v, r=r, gi=gi, st=st, sz=sz: v.tensor_copy(
                            PTp[r][:, st:st + sz, :],
                            psPTr[r][:, 1024 * gi:1024 * gi + 128 * sz].rearrange(
                                "p (a c) -> p a c", a=sz)), inc=True, drain=False)
                        ev4.append(("VE", c))
                    else:
                        b.wait("scalar", "TE", tT)
                        c = b.sdo(lambda sc, r=r, gi=gi, st=st, sz=sz: sc.copy(
                            PTp[r][:, st:st + sz, :],
                            psPTr[r][:, 1024 * gi:1024 * gi + 128 * sz].rearrange(
                                "p (a c) -> p a c", a=sz)), inc=True, drain=False)
                        ev4.append(("SE", c))
                for gi, (st, sz) in enumerate(grs):
                    sem, c = ev4[gi]
                    b.wait("tensor", sem, c)
                    def pv(t, it=it, r=r, st=st, sz=sz, gi=gi, ng=ng):
                        for kt in range(st, st + sz):
                            ins = t.matmul(psOr[r], PTp[r][:, kt, :], it["vf"](kt),
                                           start=(kt == 0), stop=(kt == nk - 1))
                        return ins
                    tPV[m] = b.tdo(pv)
                b.wait("scalar", "TE", tPV[m])
                vOut[m] = b.sdo(lambda sc, it=it, r=r: it["out"](sc, psOr[r]), inc=True, drain=False)
                if inject_t is not None:
                    inject_t(m, sExp4[m])

            for n in range(N):
                emit_S(n)
                if n >= 1:
                    emit_tail(n - 1)
                emit_vmax(n)
                emit_exp(n)
                if n >= 1:
                    emit_vsum(n - 1)
                    if inject_v is not None:
                        injV[n - 1] = inject_v(n - 1)
                    if inject_s is not None:
                        injS[n - 1] = inject_s(n - 1)
            emit_tail(N - 1)
            emit_vsum(N - 1)
            if inject_v is not None:
                injV[N - 1] = inject_v(N - 1)
            if inject_s is not None:
                injS[N - 1] = inject_s(N - 1)
            b.wait("tensor", "SE", vOut[N - 1])
            b.wait("vector", "SE", vOut[N - 1])
            b.wait("gpsimd", "SE", vOut[N - 1])
            b.wait("sync", "SE", vOut[N - 1])

        QA = [0, 1, 2, 3, 8, 9, 10, 11]
        QB = [4, 5, 6, 7, 12, 13, 14, 15]
        def qloc(h):
            return (QA.index(h), 0) if h in QA else (QB.index(h), 1)
        iters1 = []
        for h in range(H):
            g = h // 4
            qc, qh = qloc(h)
            for qi in range(NT):
                def mkout(h=h, qi=qi):
                    return lambda sc, pso: sc.copy(oall[:, qi, 64 * h:64 * (h + 1)], pso)
                iters1.append(dict(
                    lq=q1f[64 * qh:64 * qh + 64, qc, 128 * qi:128 * (qi + 1)],
                    rk=(lambda off, w, g=g: k1f[64 * (g % 2):64 * (g % 2) + 64, g // 2, off:off + w]),
                    nk=NT,
                    vf=(lambda kt, g=g: v1t[:, kt, 64 * g:64 * (g + 1)]),
                    out=mkout(),
                    sacc=sumsA[:, qi, h:h + 1]))
        attn_pipe(iters1, masked=False)
        ph3_ve = VE.n
        dbg_dump("oall1", oall[:], [128, NT, D], BF16, [("VE", ph3_ve)])
        dbg_dump("sumsA", sumsA[:], [128, NT, 16], F32, [("VE", ph3_ve)])

        b.wait("tensor", "GD", gd_wout)
        of = hf
        b.wait("sync", "VE", VE.n)
        b.dma("sync", g45a[:], ln1_g)
        b.dma("sync", g45b[:], ln1_b)
        b.wait("vector", "LD", LD.n)
        b.vdo(lambda v: v.drain(), inc=False, drain=False)
        b.vdo(lambda v: v.reciprocal(rsA[:], sumsA[:]), inc=True, drain=True)
        psX2 = PS[:, 0:1024]
        veAdd45 = [0] * NT
        for i in range(NT):
            b.vdo(lambda v, i=i: v.tensor_mul(
                e_b1[:].rearrange("p (a c) -> p a c", a=H),
                oall[:, i, :].rearrange("p (a c) -> p a c", a=H),
                rsA[:, i, :].unsqueeze(2).broadcast_to([128, H, 64])), inc=True)
            b.wait("tensor", "VE", VE.n)
            tv_a = to_featmajor(i, e_b1, of)
            if i >= 1:
                b.wait("sync", "VE", veAdd45[i - 1])
            b.dma("sync", e_xt[:], x_in[128 * i:128 * (i + 1), :])
            ldx = LD.n
            px = psX if i % 2 == 0 else psX2
            if i >= 2:
                b.wait("tensor", "VE", veAdd45[i - 2])
            def pmm(t, i=i, px=px):
                for j in range(FC):
                    for h2 in range(2):
                        ins = t.matmul(px[:, 512 * h2:512 * (h2 + 1)],
                                       of[:, j, 128 * i:128 * (i + 1)],
                                       wout_sb[:, j, 512 * h2:512 * (h2 + 1)],
                                       start=(j == 0), stop=(j == FC - 1))
                return ins
            tm = b.tdo(pmm)
            b.wait("vector", "TE", tm)
            b.wait("vector", "LD", ldx)
            veAdd45[i] = b.vdo(lambda v, px=px: v.tensor_add(e_c1[:], px, e_xt[:]), inc=True)
            b.wait("sync", "VE", veAdd45[i])
            b.dma("sync", d_x1[128 * i:128 * (i + 1), :], e_c1[:])
            ld_x1 = LD.n
            ln_plain_tile(e_c1, e_c2, 1e-5, stat0, stat1, e_xt)
            b.vdo(lambda v: v.tensor_copy(e_c3[:], e_c2[:]), inc=True)
            b.wait("sync", "VE", VE.n)
            b.dma("sync", d_y[128 * i:128 * (i + 1), :], e_c3[:])
            b.wait("vector", "LD", LD.n)
            ln_gb_tile(e_c2, e_c3, 1e-5, g45a, g45b, e_c1, s5=s45)
            b.wait("vector", "TE", tv_a)
            b.vdo(lambda v: v.tensor_copy(e_b1[:], e_c3[:]), inc=True)
            b.wait("tensor", "VE", VE.n)
            to_featmajor(i, e_b1, hf)
            b.wait("vector", "LD", LD.n)
        ph4_ld = LD.n
        dbg_dump("x1", d_x1, [T, D], F32, [("LD", ph4_ld)])
        ph5_se, ph5_ld = SE.n, LD.n
        dbg_dump("y", d_y, [T, D], F32, [("LD", ph5_ld)])
        dbg_dump("hf2", hf[:], [128, FC, T], BF16, [("SE", ph5_se)])

        b.wait("gpsimd", "SE", ph5_se)
        qkvtP = [qkvt, qkvtB]
        q2fP = [q2f, q2fB]
        k2fP = [k2f, k2fB]
        wqkvP = [wqkvh0, wqkvh1]
        b.dma("gpsimd", wqkvh0[:],
              wqkv_in[:, 0:192].rearrange("(j p) o -> p j o", p=128))
        gdW = [0] * (H + 1)
        gdW[0] = GD.n
        b.dma("gpsimd", wqkvh1[:],
              wqkv_in[:, 192:384].rearrange("(j p) o -> p j o", p=128))
        gdW[1] = GD.n
        b.wait("tensor", "GD", gdW[0])
        scopyC = [[0] * NT for _ in range(H)]
        ropeC = [[0] * NT for _ in range(H)]
        projC15 = [0] * H
        tproj = [0] * NT
        vproj = [0] * NT
        for i in range(NT):
            px = psXa if i % 2 == 0 else psXb
            if i >= 2:
                b.wait("tensor", "VE", vproj[i - 2])
            def qkvmm(t, i=i, px=px):
                for j in range(FC):
                    ins = t.matmul(px, hf[:, j, 128 * i:128 * (i + 1)],
                                   wqkvh0[:, j, :], start=(j == 0), stop=(j == FC - 1))
                return ins
            tproj[i] = b.tdo(qkvmm)
            b.wait("vector", "TE", tproj[i])
            vproj[i] = b.vdo(lambda v, i=i, px=px: v.tensor_copy(qkvt[:, i, :], px), inc=True, drain=False)
        projC15[0] = tproj[NT - 1]
        qa, qb2 = qkvt[:, :, 0:32], qkvt[:, :, 32:64]
        ka, kb = qkvt[:, :, 64:96], qkvt[:, :, 96:128]
        b.vdo(lambda v: v.tensor_mul(rs1[:], qa, rCq[:]), drain=False)
        b.vdo(lambda v: v.tensor_mul(rs2[:], qb2, rSq[:]), drain=True)
        b.vdo(lambda v: v.tensor_sub(q2r[:, :, 0:32], rs1[:], rs2[:]), drain=False)
        b.vdo(lambda v: v.tensor_mul(rs3[:], qa, rSq[:]), drain=False)
        b.vdo(lambda v: v.tensor_mul(rs4[:], qb2, rCq[:]), drain=True)
        b.vdo(lambda v: v.tensor_add(q2r[:, :, 32:64], rs3[:], rs4[:]), drain=False)
        b.vdo(lambda v: v.tensor_mul(rs1[:], ka, rCk[:]), drain=False)
        b.vdo(lambda v: v.tensor_mul(rs2[:], kb, rSk[:]), drain=True)
        b.vdo(lambda v: v.tensor_sub(k2r[:, :, 0:32], rs1[:], rs2[:]), drain=False)
        b.vdo(lambda v: v.tensor_mul(rs3[:], ka, rSk[:]), drain=False)
        b.vdo(lambda v: v.tensor_mul(rs4[:], kb, rCk[:]), drain=True)
        b.vdo(lambda v: v.tensor_add(k2r[:, :, 32:64], rs3[:], rs4[:]), inc=True)
        b.wait("tensor", "VE", VE.n)
        for i in range(NT):
            def t6(t, i=i):
                t.transpose(psT[0:64, 0:128], q2r[:, i, :], ident[:])
                return t.transpose(psT[0:64, 128:256], k2r[:, i, :], ident[:])
            tv6 = b.tdo(t6)
            b.wait("scalar", "TE", tv6)
            def cp6(s, i=i):
                s.copy(q2f[:, 128 * i:128 * (i + 1)], psT[0:64, 0:128])
                s.copy(k2f[:, 128 * i:128 * (i + 1)], psT[0:64, 128:256])
            s6 = b.sdo(cp6, inc=True)
            scopyC[0][i] = s6
            b.wait("tensor", "SE", s6)

        def stage_rope4(p, t0):
            qa = qkvtP[p][:, t0:t0 + 4, 0:32]
            qb = qkvtP[p][:, t0:t0 + 4, 32:64]
            ka = qkvtP[p][:, t0:t0 + 4, 64:96]
            kb = qkvtP[p][:, t0:t0 + 4, 96:128]
            cq, sq = rCq[:, t0:t0 + 4, :], rSq[:, t0:t0 + 4, :]
            ck, sk = rCk[:, t0:t0 + 4, :], rSk[:, t0:t0 + 4, :]
            b.vdo(lambda v: v.tensor_mul(rtmp[:, 0:4, :], qa, cq), drain=False)
            b.vdo(lambda v: v.tensor_mul(rtmp[:, 4:8, :], qb, sq), drain=False)
            b.vdo(lambda v: v.tensor_mul(rtmp[:, 8:12, :], qa, sq), drain=False)
            b.vdo(lambda v: v.tensor_mul(rtmp[:, 12:16, :], qb, cq), drain=False)
            b.vdo(lambda v: v.tensor_mul(rtmp[:, 16:20, :], ka, ck), drain=False)
            b.vdo(lambda v: v.tensor_mul(rtmp[:, 20:24, :], kb, sk), drain=False)
            b.vdo(lambda v: v.tensor_mul(rtmp[:, 24:28, :], ka, sk), drain=False)
            b.vdo(lambda v: v.tensor_mul(rtmp[:, 28:32, :], kb, ck), drain=True)
            b.vdo(lambda v: v.tensor_sub(q2r[:, t0:t0 + 4, 0:32], rtmp[:, 0:4, :], rtmp[:, 4:8, :]), drain=False)
            b.vdo(lambda v: v.tensor_add(q2r[:, t0:t0 + 4, 32:64], rtmp[:, 8:12, :], rtmp[:, 12:16, :]), drain=False)
            b.vdo(lambda v: v.tensor_sub(k2r[:, t0:t0 + 4, 0:32], rtmp[:, 16:20, :], rtmp[:, 20:24, :]), drain=False)
            b.vdo(lambda v: v.tensor_add(k2r[:, t0:t0 + 4, 32:64], rtmp[:, 24:28, :], rtmp[:, 28:32, :]),
                  inc=True, drain=False)
            return VE.n

        injVrec = [0] * (H * NT)
        ropeB = [[0] * 4 for _ in range(H)]
        tpRec = [0] * (H * NT)
        tt6Rec = [0] * (H * NT)

        def stg(m):
            h, li = m // NT, m % NT
            tj = (li - 4) % NT
            th = (h + 1) if li >= 4 else h
            return h, li, h + 1, tj, th

        def inject6_t(m, se4):
            h, li, hs, tj, th = stg(m)
            r = m % 2
            nk = li + 1
            if hs < H:
                p = hs % 2
                if li == 0:
                    b.wait("tensor", "GD", gdW[hs])
                if m >= 2 and injVrec[m - 2]:
                    b.wait("tensor", "VE", injVrec[m - 2])
                if nk >= 5:
                    b.wait("tensor", "SE", se4[1])
                def pj(t, li=li, p=p, r=r):
                    for j in range(FC):
                        ins = t.matmul(psXp[r], hf[:, j, 128 * li:128 * (li + 1)],
                                       wqkvP[p][:, j, :], start=(j == 0), stop=(j == FC - 1))
                    return ins
                tp = b.tdo(pj)
                tpRec[m] = tp
                if li == NT - 1:
                    projC15[hs] = tp
                    if hs + 1 < H:
                        b.wait("gpsimd", "TE", projC15[hs - 1])
                        b.dma("gpsimd", wqkvP[(hs + 1) % 2][:],
                              wqkv_in[:, 192 * (hs + 1):192 * (hs + 2)].rearrange(
                                  "(j p) o -> p j o", p=128))
                        gdW[hs + 1] = GD.n
            if 1 <= th < H:
                if nk >= 9:
                    b.wait("tensor", "SE", se4[2])
                b.wait("tensor", "VE", ropeB[th][tj // 4])
                def t6i(t, tj=tj, r=r):
                    t.transpose(psTq[r], q2r[:, tj, :], ident[:])
                    return t.transpose(psTk[r], k2r[:, tj, :], ident[:])
                tt6Rec[m] = b.tdo(t6i)

        def inject6_v(m):
            h, li, hs, tj, th = stg(m)
            r = m % 2
            if hs >= H:
                return 0
            p = hs % 2
            b.wait("vector", "TE", tpRec[m])
            b.vdo(lambda v, li=li, p=p, r=r: v.tensor_copy(qkvtP[p][:, li, :], psXp[r]),
                  inc=True, drain=(li % 4 == 3))
            ve_free = VE.n
            injVrec[m] = ve_free
            if li % 4 == 3:
                ropeB[hs][li // 4] = stage_rope4(p, li - 3)
            return ve_free

        def inject6_s(m):
            h, li, hs, tj, th = stg(m)
            r = m % 2
            if not (1 <= th < H):
                return 0
            b.wait("scalar", "TE", tt6Rec[m])
            pth = th % 2
            def cp6i(s, tj=tj, pth=pth, r=r):
                s.copy(q2fP[pth][:, 128 * tj:128 * (tj + 1)], psTq[r])
                return s.copy(k2fP[pth][:, 128 * tj:128 * (tj + 1)], psTk[r])
            sc6 = b.sdo(cp6i, inc=True, drain=False)
            scopyC[th][tj] = sc6
            return sc6

        iters2 = []
        for h in range(H):
            p = h % 2
            for qi in range(NT):
                def mkout2(h=h, qi=qi):
                    return lambda sc, pso: sc.copy(oall[:, qi, 64 * h:64 * (h + 1)], pso)
                iters2.append(dict(
                    sacc=sumsB[:, qi, h:h + 1],
                    lq=q2fP[p][:, 128 * qi:128 * (qi + 1)],
                    rk=(lambda off, w, p=p: k2fP[p][:, off:off + w]),
                    nk=qi + 1,
                    vf=(lambda kt, p=p: qkvtP[p][:, kt, 128:192]),
                    out=mkout2(),
                    sw=(None if h == 0 else (lambda h=h, qi=qi: scopyC[h][qi]))))
        attn_pipe(iters2, masked=True, inject_t=inject6_t,
                  inject_v=inject6_v, inject_s=inject6_s)
        ph6_ve = VE.n
        dbg_dump("oall2", oall[:], [128, NT, D], BF16, [("VE", ph6_ve)])
        dbg_dump("qkvt", qkvt[:], [128, NT, 192], BF16, [("VE", ph6_ve)])
        dbg_dump("q2r", q2r[:], [128, NT, HD], BF16, [("VE", ph6_ve)])
        dbg_dump("k2r", k2r[:], [128, NT, HD], BF16, [("VE", ph6_ve)])
        dbg_dump("q2fd", q2f[:], [64, T], BF16, [("VE", ph6_ve)])
        dbg_dump("k2fd", k2f[:], [64, T], BF16, [("VE", ph6_ve)])

        b.wait("gpsimd", "VE", ph6_ve)
        b.dma("gpsimd", wproj_sb[:], wproj_in.rearrange("(j p) o -> p j o", p=128))
        gd_wp = GD.n
        of2 = hf
        b.vdo(lambda v: v.drain(), inc=False, drain=False)
        b.vdo(lambda v: v.reciprocal(rsB[:], sumsB[:]), inc=True, drain=True)
        for i in range(NT):
            b.vdo(lambda v, i=i: v.tensor_mul(
                e_b1[:].rearrange("p (a c) -> p a c", a=H),
                oall[:, i, :].rearrange("p (a c) -> p a c", a=H),
                rsB[:, i, :].unsqueeze(2).broadcast_to([128, H, 64])), inc=True)
            b.wait("tensor", "VE", VE.n)
            to_featmajor(i, e_b1, of2)
        b.wait("tensor", "GD", gd_wp)
        b.wait("sync", "VE", VE.n)
        for i in range(NT):
            b.dma("sync", e_xt[:], d_y[128 * i:128 * (i + 1), :])
            ldy = LD.n
            def pmm2(t, i=i):
                for j in range(FC):
                    for h2 in range(2):
                        ins = t.matmul(psX[:, 512 * h2:512 * (h2 + 1)],
                                       of2[:, j, 128 * i:128 * (i + 1)],
                                       wproj_sb[:, j, 512 * h2:512 * (h2 + 1)],
                                       start=(j == 0), stop=(j == FC - 1))
                return ins
            tm2 = b.tdo(pmm2)
            b.wait("vector", "TE", tm2)
            b.wait("vector", "LD", ldy)
            b.vdo(lambda v: v.tensor_add(e_c1[:], psX, e_xt[:]), inc=True)
            b.wait("sync", "VE", VE.n)
            b.dma("sync", d_x2[128 * i:128 * (i + 1), :], e_c1[:])
            b.wait("vector", "LD", LD.n)
            b.wait("tensor", "LD", LD.n)
        ph6f_ld = LD.n
        dbg_dump("x2", d_x2, [T, D], F32, [("LD", ph6f_ld)])

        b.wait("gpsimd", "TE", TE.n)
        w1v = w1_in.rearrange("(j p) m -> p j m", p=128)
        b.dma("gpsimd", w1sb[:, 2:8, :], w1v[:, 2:8, :])
        w2v = w2_in.rearrange("(j p) o -> p j o", p=128)
        for _wc in range(3):
            b.dma("gpsimd", w2sb[:, 8 * _wc:8 * (_wc + 1), :], w2v[:, 8 * _wc:8 * (_wc + 1), :])
        b.wait("sync", "LD", ph6f_ld)
        b.wait("sync", "VE", VE.n)
        b.dma("sync", g_v0[:], ln2_g)
        b.dma("sync", g_v1[:], ln2_b)
        b.wait("vector", "LD", LD.n)
        for i in range(NT):
            b.dma("sync", e_xt[:], d_x2[128 * i:128 * (i + 1), :])
            b.wait("vector", "LD", LD.n)
            ln_gb_tile(e_xt, e_c3, 1e-5, g_v0, g_v1, e_c1)
            b.vdo(lambda v: v.tensor_copy(e_b1[:], e_c3[:]), inc=True)
            b.wait("sync", "VE", VE.n)
            b.dma("sync", d_h2[128 * i:128 * (i + 1), :], e_b1[:])
            if i % 4 == 3:
                cb = i // 4
                b.wait("gpsimd", "LD", LD.n)
                b.coll(d_h2[512 * cb:512 * (cb + 1), :].opt(), ag_h2[cb].opt())
            b.wait("tensor", "VE", VE.n)
            se = to_featmajor(i, e_b1, h2f)
            def lmm(t, i=i):
                for j in range(FC):
                    ins = t.matmul(psL, h2f[:, j, 128 * i:128 * (i + 1)], wg_sb[:, j, :],
                                   start=(j == 0), stop=(j == FC - 1))
                return ins
            tl = b.tdo(lmm)
            b.wait("vector", "TE", tl)
            b.vdo(lambda v, i=i: v.tensor_copy(lgt[:, i, :], psL), inc=True)
            b.wait("tensor", "VE", VE.n)
            b.wait("vector", "LD", LD.n)
        b.wait("gpsimd", "TE", TE.n)
        b.dma("gpsimd", w2sb[:, 24:32, :], w2v[:, 24:32, :])
        b.dma("gpsimd", w1sb[:, 0:2, :], w1v[:, 0:2, :])
        st_w12 = GD.n
        i8bc = iota8[:].unsqueeze(1).broadcast_to([128, NT, 8])
        b.vdo(lambda v: v.reduce_max(stat0[:], lgt[:], axis=AX.X))
        b.vdo(lambda v: v.tensor_sub(t2a[:], lgt[:], stat0[:].unsqueeze(2).broadcast_to([128, NT, 8])), inc=True)
        b.wait("scalar", "VE", b.s["VE"].n)
        sg = b.sdo(lambda s: s.activation(out=t2b[:], in_=t2a[:], func=AF.Exp), inc=True)
        b.wait("vector", "SE", sg)
        b.vdo(lambda v: v.reduce_sum(stat1[:], t2b[:], axis=AX.X))
        b.vdo(lambda v: v.reciprocal(stat2[:], stat1[:]))
        b.vdo(lambda v: v.tensor_mul(gts[:], t2b[:], stat2[:].unsqueeze(2).broadcast_to([128, NT, 8])))
        b.vdo(lambda v: v.reduce_max(g1t[:], gts[:], axis=AX.X))
        b.vdo(lambda v: v.tensor_tensor(t2a[:], gts[:], g1t[:].unsqueeze(2).broadcast_to([128, NT, 8]), ALU.is_ge))
        b.vdo(lambda v: v.tensor_mul(t2b[:], t2a[:], i8bc))
        b.vdo(lambda v: v.tensor_scalar(e_s4[:], t2a[:], -8.0, 8.0, ALU.mult, ALU.add))
        b.vdo(lambda v: v.tensor_add(t2a[:], t2b[:], e_s4[:]))
        b.vdo(lambda v: v.tensor_reduce(e1t[:], t2a[:], axis=AX.X, op=ALU.min))
        b.vdo(lambda v: v.tensor_tensor(t2a[:], i8bc, e1t[:].unsqueeze(2).broadcast_to([128, NT, 8]), ALU.is_equal))
        b.vdo(lambda v: v.scalar_tensor_tensor(t2b[:], t2a[:], -1e9, gts[:], ALU.mult, ALU.add))
        b.vdo(lambda v: v.reduce_max(g2t[:], t2b[:], axis=AX.X))
        b.vdo(lambda v: v.tensor_tensor(t2a[:], t2b[:], g2t[:].unsqueeze(2).broadcast_to([128, NT, 8]), ALU.is_ge))
        b.vdo(lambda v: v.tensor_mul(t2b[:], t2a[:], i8bc))
        b.vdo(lambda v: v.tensor_scalar(e_s4[:], t2a[:], -8.0, 8.0, ALU.mult, ALU.add))
        b.vdo(lambda v: v.tensor_add(t2a[:], t2b[:], e_s4[:]))
        b.vdo(lambda v: v.tensor_reduce(e2t[:], t2a[:], axis=AX.X, op=ALU.min), inc=True)
        ve_top2 = VE.n
        dbg_dump("gates", gts[:], [128, NT, 8], F32, [("VE", ve_top2)])
        dbg_dump("e1", e1t[:], [128, NT], F32, [("VE", ve_top2)])
        dbg_dump("e2", e2t[:], [128, NT], F32, [("VE", ve_top2)])
        b.vdo(lambda v: v.tensor_copy(e_b1[:, 0:16], e1t[:]))
        b.vdo(lambda v: v.tensor_copy(e_b1[:, 16:32], e2t[:]), inc=True)
        b.wait("tensor", "VE", VE.n)
        tpy = b.tdo(lambda t: t.transpose(psT[0:32, 0:128], e_b1[:, 0:32], ident[:]))
        b.wait("scalar", "TE", tpy)
        spy = b.sdo(lambda s: s.copy(e_b2[0:32, 0:128], psT[0:32, 0:128]), inc=True)
        b.wait("sync", "SE", spy)
        b.dma("sync", d_ep1, e_b2[0:16, 0:128])
        b.dma("sync", d_ep2, e_b2[16:32, 0:128])
        ld_pay = LD.n
        b.wait("gpsimd", "LD", ld_pay)
        b.coll(d_ep1.opt(), ag_e1.opt())
        b.coll(d_ep2.opt(), ag_e2.opt())
        cc1 = CC.n


        b.wait("sync", "CC", cc1)
        e1view = ag_e1.rearrange("c a b -> (c a) b")
        e2view = ag_e2.rearrange("c a b -> (c a) b")
        b.dma("sync", e1g[:], e1view, transpose=True)
        b.dma("sync", e2g[:], e2view, transpose=True)
        b.wait("vector", "LD", LD.n)
        for e in range(E):
            b.vdo(lambda v, e=e: v.tensor_single_scalar(mgs1[:], e1g[:], float(e), ALU.is_equal))
            b.vdo(lambda v, e=e: v.tensor_single_scalar(mgs2[:], e2g[:], float(e), ALU.is_equal))
            b.vdo(lambda v, e=e: v.tensor_add(maskf[:, e, :], mgs1[:], mgs2[:]))
        b.vdo(lambda v: v.tensor_copy(maskb[:], maskf[:]), inc=True)
        b.wait("tensor", "VE", VE.n)
        mflat = maskb[:].rearrange("p e c -> p (e c)")
        def rmm(t):
            t.matmul(psPP[:, 0:512], ltri[:], mflat[:, 0:512], start=True, stop=True)
            t.matmul(psPP[:, 512:1024], ltri[:], mflat[:, 512:1024], start=True, stop=True)
            t.matmul(psCS[:, 0:512], ones[:], mflat[:, 0:512], start=True, stop=True)
            return t.matmul(psCS[:, 512:1024], ones[:], mflat[:, 512:1024], start=True, stop=True)
        trm = b.tdo(rmm)
        b.wait("vector", "TE", trm)
        b.vdo(lambda v: v.tensor_copy(ppref[:], psPP.rearrange("p (e c) -> p e c", e=E)))
        b.vdo(lambda v: v.tensor_copy(colsum[:], psCS.rearrange("p (e c) -> p e c", e=E)), inc=True)
        b.wait("tensor", "VE", VE.n)
        b.vdo(lambda v: v.memset(csh1[:, :, 0:1], 0.0))
        b.vdo(lambda v: v.tensor_copy(csh1[:, :, 1:], colsum[:, :, :-1]))
        src, dst2 = csh1, csh2
        for sh in [1, 2, 4, 8, 16, 32, 64]:
            def step(v, s=src, d=dst2, sh=sh):
                v.tensor_copy(d[:, :, :sh], s[:, :, :sh])
                v.tensor_add(d[:, :, sh:], s[:, :, sh:], s[:, :, :-sh])
            b.vdo(step)
            src, dst2 = dst2, src
        colpre = src
        b.vdo(lambda v, cp=colpre: v.tensor_add(posg[:], ppref[:], cp[:]))
        pview = posg[:].rearrange("p e (B j) -> p e j B", B=8)
        oBbc = ownB[:].unsqueeze(1).unsqueeze(2).broadcast_to([128, E, NT, 8])
        b.vdo(lambda v: v.tensor_mul(csh1[:].rearrange("p e (j B) -> p e j B", j=16), pview, oBbc))
        b.vdo(lambda v: v.reduce_sum(ownps[:], csh1[:].rearrange("p e (j B) -> p e j B", j=16), axis=AX.X))
        ownT = ownps[:].rearrange("p e j -> p j e")
        b.vdo(lambda v: v.tensor_tensor(t2a[:], i8bc, e1t[:].unsqueeze(2).broadcast_to([128, NT, 8]), ALU.is_equal))
        b.vdo(lambda v: v.tensor_mul(t2b[:], t2a[:], ownT))
        b.vdo(lambda v: v.reduce_sum(pos1[:], t2b[:], axis=AX.X))
        b.vdo(lambda v: v.tensor_tensor(t2a[:], i8bc, e2t[:].unsqueeze(2).broadcast_to([128, NT, 8]), ALU.is_equal))
        b.vdo(lambda v: v.tensor_mul(t2b[:], t2a[:], ownT))
        b.vdo(lambda v: v.reduce_sum(pos2[:], t2b[:], axis=AX.X))
        b.vdo(lambda v: v.tensor_single_scalar(kp1[:], pos1[:], float(CAP), ALU.is_lt))
        b.vdo(lambda v: v.tensor_single_scalar(kp2[:], pos2[:], float(CAP), ALU.is_lt))
        b.vdo(lambda v: v.tensor_mul(sc1[:], g1t[:], kp1[:]))
        b.vdo(lambda v: v.tensor_mul(sc2[:], g2t[:], kp2[:]))
        def emit_cidx(post, et, idxt):
            b.vdo(lambda v: v.tensor_single_scalar(stat0[:], post[:], float(CAP - 1), ALU.min))
            b.vdo(lambda v: v.tensor_single_scalar(stat1[:], stat0[:], 1024.0, ALU.is_ge))
            b.vdo(lambda v: v.tensor_single_scalar(stat2[:], stat0[:], 2048.0, ALU.is_ge))
            b.vdo(lambda v: v.tensor_add(stat3[:], stat1[:], stat2[:]))
            b.vdo(lambda v: v.tensor_single_scalar(stat1[:], stat0[:], 3072.0, ALU.is_ge))
            b.vdo(lambda v: v.tensor_add(stat2[:], stat3[:], stat1[:]))
            b.vdo(lambda v: v.scalar_tensor_tensor(stat3[:], stat2[:], 7168.0, stat0[:], ALU.mult, ALU.add))
            b.vdo(lambda v: v.scalar_tensor_tensor(stat1[:], et[:], 1024.0, stat3[:], ALU.mult, ALU.add))
            b.vdo(lambda v: v.tensor_copy(idxt[:], stat1[:]))
        emit_cidx(pos1, e1t, idx1)
        emit_cidx(pos2, e2t, idx2)
        b.vdo(lambda v: v.engine_nop(), inc=True)
        ve_own = VE.n
        dbg_dump("pos1", pos1[:], [128, NT], F32, [("VE", ve_own)])
        dbg_dump("sc1", sc1[:], [128, NT], F32, [("VE", ve_own)])
        dbg_dump("sc2", sc2[:], [128, NT], F32, [("VE", ve_own)])
        mTv = maskf[:].rearrange("p e c -> p c e")
        pTv = ppref[:].rearrange("p e c -> p c e")
        cTv = colpre[:].rearrange("p e c -> p c e")
        esbc = esel[:].unsqueeze(1).broadcast_to([128, 128, 8])
        b.vdo(lambda v: v.tensor_mul(csh1[:].rearrange("p e c -> p c e"), mTv, esbc))
        b.vdo(lambda v: v.reduce_sum(mymask[:], csh1[:].rearrange("p e c -> p c e"), axis=AX.X))
        b.vdo(lambda v: v.tensor_mul(csh1[:].rearrange("p e c -> p c e"), pTv, esbc))
        b.vdo(lambda v: v.reduce_sum(myppref[:], csh1[:].rearrange("p e c -> p c e"), axis=AX.X))
        b.vdo(lambda v: v.tensor_mul(csh1[:].rearrange("p e c -> p c e"), cTv, esbc))
        b.vdo(lambda v: v.reduce_sum(mycolpre[:], csh1[:].rearrange("p e c -> p c e"), axis=AX.X), inc=True)
        b.wait("tensor", "VE", VE.n)
        for c in range(128):
            b.vdo(lambda v, c=c: v.tensor_tensor(ohf[:], myppref[:, c:c + 1].broadcast_to([128, 128]),
                                                 iotarow[:], ALU.is_equal))
            b.vdo(lambda v, c=c: v.tensor_mul(mgs1[:], ohf[:], mymask[:, c:c + 1].broadcast_to([128, 128])))
            b.vdo(lambda v: v.tensor_copy(ohbf[:], mgs1[:]), inc=True)
            b.wait("tensor", "VE", VE.n)
            tc = b.tdo(lambda t, c=c: t.matmul(psPR[:, c:c + 1], ohbf[:], pcol[:],
                                               start=True, stop=True))
            b.wait("vector", "TE", tc)
        b.vdo(lambda v: v.tensor_copy(prow_sb[:], psPR[:, 0:128]), inc=True)
        b.wait("sync", "VE", VE.n)
        b.dma("sync", d_prow, prow_sb[:])
        ld_prow = LD.n
        b.wait("vector", "LD", ld_prow)
        cpbc = mycolpre[:].unsqueeze(1).broadcast_to([128, 32, 128])
        slbc = iotaslot[:].unsqueeze(2).broadcast_to([128, 32, 128])
        b.vdo(lambda v: v.tensor_tensor(cmpb[:], cpbc, slbc, ALU.is_le))
        b.vdo(lambda v: v.reduce_sum(colS[:], cmpb[:], axis=AX.X))
        b.vdo(lambda v: v.tensor_scalar_add(rkS[:], colS[:], -1.0))
        b.vdo(lambda v: v.tensor_copy(colS[:], rkS[:]), inc=True)
        b.vdo(lambda v: v.tensor_mul(srA[:], cmpb[:], cpbc))
        b.vdo(lambda v: v.reduce_max(cbS[:], srA[:], axis=AX.X))
        b.vdo(lambda v: v.tensor_sub(rkS[:], iotaslot[:], cbS[:]))
        b.vdo(lambda v: v.tensor_single_scalar(cbS[:], rkS[:], 127.0, ALU.min))
        b.vdo(lambda v: v.tensor_single_scalar(rkS[:], cbS[:], 0.0, ALU.max))
        b.vdo(lambda v: v.scalar_tensor_tensor(cbS[:], rkS[:], 128.0, colS[:], ALU.mult, ALU.add))
        b.vdo(lambda v: v.tensor_copy(fidx[:], cbS[:]), inc=True)
        ve_fidx = VE.n
        b.wait("gpsimd", "VE", ve_fidx)
        b.wait("gpsimd", "LD", ld_prow)
        for j in range(32):
            b.idma(prowg[:, j:j + 1], None, d_prow.rearrange("a b -> (a b)").rearrange("(n o) -> n o", o=1),
                   IndirectOffsetOnAxis(fidx[:, j:j + 1], 0))
        gd_prowg = GD.n
        b.wait("vector", "GD", gd_prowg)
        b.vdo(lambda v: v.scalar_tensor_tensor(cbS[:], colS[:], 128.0, prowg[:], ALU.mult, ALU.add))
        for k in range(1, 32):
            b.vdo(lambda v, k=k: v.tensor_single_scalar(srA[:, :, k], cbS[:], 512.0 * k, ALU.is_ge),
                  drain=(k == 31))
        b.vdo(lambda v: v.reduce_sum(rkS[:], srA[:, :, 1:32], axis=AX.X), drain=True)
        for k in range(1, 8):
            b.vdo(lambda v, k=k: v.tensor_single_scalar(srA[:, :, 32 + k], cbS[:], 2048.0 * k, ALU.is_ge),
                  drain=(k == 7))
        b.vdo(lambda v: v.reduce_sum(colS[:], srA[:, :, 33:40], axis=AX.X), drain=True)
        b.vdo(lambda v: v.scalar_tensor_tensor(srA[:, 0, 0:32], rkS[:], 3584.0, cbS[:], ALU.mult, ALU.add), drain=True)
        b.vdo(lambda v: v.scalar_tensor_tensor(srA[:, 0, 32:64], colS[:], -15872.0, srA[:, 0, 0:32], ALU.mult, ALU.add), drain=True)
        b.vdo(lambda v: v.tensor_copy(tokl[:], srA[:, 0, 32:64]), inc=True)
        ve_tokl = VE.n
        dbg_dump("tokl", tokl[:], [128, 32], I32, [("VE", ve_tokl)])

        b.wait("sync", "VE", ve_tokl)
        b.dma("sync", b2sb[:], b2_in)
        ld_b2 = LD.n
        b.wait("gpsimd", "VE", ve_tokl)
        b.wait("gpsimd", "CC", cc1)
        b.wait("tensor", "GD", st_w12)
        b.wait("vector", "LD", ld_b2)
        xgp = [xg, xg2]
        xfp = [xf, xf2]
        gX = [0] * 33
        tXf = [0] * 33
        seXf = [0] * 33
        teP1last = [0] * 33
        prevAdd = [0, 0]
        agh = ag_h2.rearrange("c e t d -> (c e t) d")
        gX[0] = b.idma(xgp[0][:], None, agh, IndirectOffsetOnAxis(tokl[:, 0:1], 0))
        for r in range(32):
            rp = r % 2
            if r + 1 < 32:
                if r >= 1:
                    b.wait("gpsimd", "TE", tXf[r - 1])
                gX[r + 1] = b.idma(xgp[(r + 1) % 2][:], None, agh,
                                   IndirectOffsetOnAxis(tokl[:, r + 1:r + 2], 0))
            b.wait("tensor", "GD", gX[r])
            if r >= 2:
                b.wait("tensor", "SE", seXf[r - 2])
            def xtp(t, rp=rp):
                for j in range(FC):
                    ins = t.transpose(psT[:, 128 * j:128 * (j + 1)],
                                      xgp[rp][:, 128 * j:128 * (j + 1)], ident[:])
                return ins
            tXf[r] = b.tdo(xtp)
            b.wait("scalar", "TE", tXf[r])
            seXf[r] = b.sdo(lambda sc, rp=rp: sc.copy(xfp[rp][:], psT[:].rearrange("p (c q) -> p c q", c=FC)),
                            inc=True, drain=False)
            b.wait("tensor", "SE", seXf[r])
            psHp = [psHm, PS[:, 2048:3072]]
            po = psOm
            hscp = [hsc, hsc2]
            hmbp = [hmb, hmb2]
            thP1 = [0] * 4
            veAdd = [0] * 4
            seG = [0] * 4

            def moe_p1(sub):
                sp = sub % 2
                if sub >= 2:
                    b.wait("tensor", "VE", veAdd[sub - 2])
                else:
                    b.wait("tensor", "VE", prevAdd[sub])
                def h1mm(t, sub=sub, rp=rp, sp=sp):
                    ins = None
                    for m in range(8):
                        mid = sub * 8 + m
                        for j in range(FC):
                            ins = t.matmul(psHp[sp][:, 128 * m:128 * (m + 1)],
                                           w1sb[:, j, 128 * mid:128 * (mid + 1)],
                                           xfp[rp][:, j, :], start=(j == 0), stop=(j == FC - 1))
                    return ins
                thP1[sub] = b.tdo(h1mm)
                b.wait("vector", "TE", thP1[sub])
                veAdd[sub] = b.vdo(lambda v, sub=sub, sp=sp: v.tensor_add(
                    hscp[sp][:].rearrange("p (a c) -> p a c", a=8),
                    psHp[sp].rearrange("p (a c) -> p a c", a=8),
                    b1sb[:, 8 * sub:8 * (sub + 1)].unsqueeze(2).broadcast_to([128, 8, 128])),
                    inc=True, drain=False)
                b.wait("scalar", "VE", veAdd[sub])
                seG[sub] = b.sdo(lambda s, sp=sp: s.activation(
                    out=hmbp[sp][:], in_=hscp[sp][:].rearrange("p (a c) -> p a c", a=8),
                    func=AF.Gelu), inc=True, drain=False)

            def moe_p2(sub):
                sp = sub % 2
                b.wait("tensor", "SE", seG[sub])
                def h2mm(t, sub=sub, sp=sp, po=po):
                    ins = None
                    for m in range(8):
                        mid = sub * 8 + m
                        for oc in range(2):
                            ins = t.matmul(po[:, 512 * oc:512 * (oc + 1)],
                                           hmbp[sp][:, m, :], w2sb[:, mid, 512 * oc:512 * (oc + 1)],
                                           start=(sub == 0 and m == 0), stop=(sub == 3 and m == 7))
                    return ins
                return b.tdo(h2mm)

            moe_p1(0)
            moe_p1(1)
            moe_p2(0)
            moe_p1(2)
            moe_p2(1)
            moe_p1(3)
            moe_p2(2)
            t2_ = moe_p2(3)
            prevAdd = [veAdd[2], veAdd[3]]
            b.wait("vector", "TE", t2_)
            teP1last[r] = TE.n
            b.vdo(lambda v, po=po: v.scalar_tensor_tensor(outt[:], po[:, 0:1024], 1.0, b2sb[:], ALU.mult, ALU.add),
                  inc=True, drain=False)
            b.wait("sync", "VE", VE.n)
            b.dma("sync", d_exp[128 * r:128 * (r + 1), :], outt[:])
            b.wait("vector", "LD", LD.n)
            if r % 8 == 7:
                cblk = r // 8
                b.wait("gpsimd", "LD", LD.n)
                b.coll(d_exp[1024 * cblk:1024 * (cblk + 1), :].opt(),
                       ag_exp[cblk].opt())
        cc2 = CC.n
        dbg_dump("expown", d_exp, [CAP, D], BF16, [("CC", cc2)])

        b.wait("gpsimd", "CC", cc2)
        for i in range(NT):
            agx = ag_exp.rearrange("c e t d -> (c e t) d")
            ga = b.idma(cA1[:], None, agx, IndirectOffsetOnAxis(idx1[:, i:i + 1], 0))
            gb = b.idma(cA2[:], None, agx, IndirectOffsetOnAxis(idx2[:, i:i + 1], 0))
            b.dma("sync", cx1[:], d_x1[128 * i:128 * (i + 1), :])
            b.dma("sync", cx2[:], d_x2[128 * i:128 * (i + 1), :])
            ldc = LD.n
            b.wait("vector", "GD", gb)
            b.wait("vector", "LD", ldc)
            b.vdo(lambda v, i=i: v.tensor_scalar_mul(ct1[:], cA1[:], sc1[:, i:i + 1]))
            b.vdo(lambda v, i=i: v.scalar_tensor_tensor(ct2[:], cA2[:], sc2[:, i:i + 1], ct1[:],
                                                        ALU.mult, ALU.add))
            b.vdo(lambda v: v.tensor_add(ct1[:], cx1[:], cx2[:]))
            b.vdo(lambda v: v.tensor_add(co[:], ct1[:], ct2[:]), inc=True)
            b.wait("sync", "VE", VE.n)
            b.dma("sync", out_o[128 * i:128 * (i + 1), :], co[:], sem="ST")
            b.wait("vector", "ST", ST.n)
            b.wait("gpsimd", "ST", ST.n)
        b.wait("sync", "ST", ST.n)

        blk = nc.Block()
        blk.__enter__()

        @blk.sync
        def _(e):
            for fn in b.q["sync"]:
                fn(e)

        @blk.tensor
        def _(e):
            for fn in b.q["tensor"]:
                fn(e)

        @blk.vector
        def _(e):
            for fn in b.q["vector"]:
                fn(e)

        @blk.scalar
        def _(e):
            for fn in b.q["scalar"]:
                fn(e)

        @blk.gpsimd
        def _(e):
            for fn in b.q["gpsimd"]:
                fn(e)

        blk.__exit__(None, None, None)
        nc.all_engine_barrier()
        semnums = sorted(sm.h.num for sm in b.s.values())
        srange = range(min(semnums), max(semnums) + 1)
        nc.gpsimd.dma_reset(srange)
        nc.gpsimd.sem_clear(srange)
        nc.all_engine_barrier()

    return nc



def _consts():
    c = {}
    c["ident_in"] = np.eye(128, dtype=ml_dtypes.bfloat16)
    c["ltri_in"] = (np.arange(128)[:, None] < np.arange(128)[None, :]).astype(ml_dtypes.bfloat16)
    c["ones_in"] = np.ones((128, 128), ml_dtypes.bfloat16)
    cm = np.where(np.arange(128)[None, :] <= np.arange(128)[:, None], 0.0, -1e9)
    c["cmask_in"] = cm.astype(np.float32).astype(ml_dtypes.bfloat16)
    inv = (1.0 / 10000.0 ** (np.arange(0, HD, 2, dtype=np.float64) / HD))
    tt = np.arange(T, dtype=np.float64)[:, None] * inv[None, :]
    cosf = np.cos(tt).astype(np.float32)
    sinf = np.sin(tt).astype(np.float32)
    def tokmaj(a):
        return np.ascontiguousarray(a.reshape(NT, 128, 32).transpose(1, 0, 2))
    c["rCq_in"] = (tokmaj(cosf) * 0.125).astype(ml_dtypes.bfloat16)
    c["rSq_in"] = (tokmaj(sinf) * 0.125).astype(ml_dtypes.bfloat16)
    c["rCk_in"] = tokmaj(cosf).astype(ml_dtypes.bfloat16)
    c["rSk_in"] = tokmaj(sinf).astype(ml_dtypes.bfloat16)
    c["iota8_in"] = np.broadcast_to(np.arange(8, dtype=np.float32), (128, 8)).copy()
    c["iotarow_in"] = np.broadcast_to(np.arange(128, dtype=np.float32), (128, 128)).copy()
    c["iotaslot_in"] = (np.arange(32, dtype=np.float32)[None, :] * 128
                        + np.arange(128, dtype=np.float32)[:, None]).astype(np.float32)
    c["pcol_in"] = np.arange(128, dtype=np.float32).reshape(128, 1).astype(ml_dtypes.bfloat16)
    return c


def _repl(v, n=128):
    return np.ascontiguousarray(np.broadcast_to(np.asarray(v, np.float32), (n, len(v))))


_NC_CACHE = {}


def kernel(L0_attn_x, L0_attn_ln_scale, L0_attn_ln_bias, L0_attn_Wq, L0_attn_Wkv,
           L0_attn_qn_scale, L0_attn_qn_bias, L0_attn_kn_scale, L0_attn_kn_bias,
           L0_attn_Wout, L0_ffn_ln1_g, L0_ffn_ln1_b, L0_ffn_Wqkv, L0_ffn_Wproj,
           L0_ffn_ln2_g, L0_ffn_ln2_b, L0_ffn_Wgate, L0_ffn_W1, L0_ffn_b1,
           L0_ffn_W2, L0_ffn_b2, _dbg=(), _build_only=False):
    x = np.ascontiguousarray(np.asarray(L0_attn_x, np.float32))
    consts = _consts()
    shared = dict(consts)
    wq = np.asarray(L0_attn_Wq, np.float32)
    _QA = [0, 1, 2, 3, 8, 9, 10, 11]
    _QB = [4, 5, 6, 7, 12, 13, 14, 15]
    blocks = []
    for j in range(8):
        blocks.append(wq[:, 64 * _QA[j]:64 * _QA[j] + 64])
        blocks.append(wq[:, 64 * _QB[j]:64 * _QB[j] + 64])
    shared["wq_in"] = np.ascontiguousarray(np.concatenate(blocks, axis=1))
    shared["wkv_in"] = np.asarray(L0_attn_Wkv, np.float32)
    shared["wout_in"] = np.asarray(L0_attn_Wout, np.float32)
    shared["wqkv_in"] = np.asarray(L0_ffn_Wqkv, np.float32)
    shared["wproj_in"] = np.asarray(L0_ffn_Wproj, np.float32)
    shared["wgate_in"] = np.asarray(L0_ffn_Wgate, np.float32)
    shared["lnA_g"] = _repl(L0_attn_ln_scale)
    shared["lnA_b"] = _repl(L0_attn_ln_bias)
    shared["ln1_g"] = _repl(L0_ffn_ln1_g)
    shared["ln1_b"] = _repl(L0_ffn_ln1_b)
    shared["ln2_g"] = _repl(L0_ffn_ln2_g)
    shared["ln2_b"] = _repl(L0_ffn_ln2_b)
    shared["qn_s"] = _repl(np.tile(np.asarray(L0_attn_qn_scale, np.float32), H))
    shared["qn_b"] = _repl(np.tile(np.asarray(L0_attn_qn_bias, np.float32), H))
    shared["kn_s"] = _repl(np.tile(np.asarray(L0_attn_kn_scale, np.float32), G4))
    shared["kn_b"] = _repl(np.tile(np.asarray(L0_attn_kn_bias, np.float32), G4))

    W1 = np.asarray(L0_ffn_W1, np.float32)
    W2 = np.asarray(L0_ffn_W2, np.float32)
    b1 = np.asarray(L0_ffn_b1, np.float32)
    b2 = np.asarray(L0_ffn_b2, np.float32)

    in_maps = []
    for ccore in range(8):
        m = dict(shared)
        m["x_in"] = np.ascontiguousarray(x[ccore])
        m["w1_in"] = np.ascontiguousarray(W1[ccore])
        m["w2_in"] = np.ascontiguousarray(W2[ccore])
        m["b1_in"] = np.ascontiguousarray(b1[ccore].reshape(32, 128).T)
        m["b2_in"] = _repl(b2[ccore])
        oh = np.zeros((128, 8), np.float32)
        oh[:, ccore] = 1.0
        m["esel_in"] = oh.copy()
        m["ownB_in"] = oh.copy()
        in_maps.append(m)

    key = tuple(sorted(_dbg))
    if key not in _NC_CACHE:
        _NC_CACHE[key] = build_nc(dbg=_dbg)
    nc = _NC_CACHE[key]
    if _build_only:
        return nc, in_maps
    res = run_bass_kernel_spmd(nc, in_maps, core_ids=list(range(8)))
    out = np.stack([np.asarray(res.results[i]["out_o"], np.float32) for i in range(8)])
    if _dbg:
        return out, res
    return out


# version 18 (final)
# speedup vs baseline: 1.0153x; 1.0153x over previous
import numpy as np
import ml_dtypes
import concourse.bass as bass
import concourse.mybir as mybir
from concourse.bass_core_rust import IndirectOffsetOnAxis
from concourse.bass_utils import run_bass_kernel_spmd

F32 = mybir.dt.float32
BF16 = mybir.dt.bfloat16
I32 = mybir.dt.int32
AF = mybir.ActivationFunctionType
ALU = mybir.AluOpType
AX = mybir.AxisListType

T, NT, D, FC = 2048, 16, 1024, 8
H, HD, G4 = 16, 64, 4
NG, CAP, E = 16384, 4096, 8
KB = 1024
BASE = 20 * KB
OA, OB, OC, OD, OE, OF = (BASE + 0, BASE + 32 * KB, BASE + 64 * KB,
                          BASE + 96 * KB, BASE + 128 * KB, BASE + 160 * KB)
OG = OF
OH = BASE + 192 * KB


class Sem:
    def __init__(self, nc, ctx, name):
        self.h = ctx.enter_context(nc.semaphore(name))
        self.n = 0


class Build:
    def __init__(self, nc, ctx, dbg=()):
        self.nc, self.ctx, self.dbg = nc, ctx, set(dbg)
        self.q = {k: [] for k in ["sync", "tensor", "vector", "scalar", "gpsimd"]}
        self.s = {k: Sem(nc, ctx, "s" + k) for k in
                  ["LD", "GD", "CC", "TE", "VE", "SE", "GE", "ST"]}

    def wait(self, eng, sem, val):
        h = self.s[sem].h
        self.q[eng].append(lambda e, h=h, v=val: e.wait_ge(h, v))

    def _do(self, eng, semname, fn, inc, drain=True):
        s = self.s[semname]
        if inc:
            s.n += 1
        def run(e, fn=fn, inc=inc, h=s.h, drain=drain):
            ins = fn(e)
            if drain:
                ins = e.drain()
            if inc:
                ins.then_inc(h, 1)
        self.q[eng].append(run)
        return s.n

    def vdo(self, fn, inc=False, drain=True):
        return self._do("vector", "VE", fn, inc, drain)

    def sdo(self, fn, inc=False, drain=True):
        return self._do("scalar", "SE", fn, inc, drain)

    def gdo(self, fn, inc=False, drain=True):
        return self._do("gpsimd", "GE", fn, inc, drain)

    def tdo(self, fn, inc=True):
        s = self.s["TE"]
        if inc:
            s.n += 1
        def run(t, fn=fn, inc=inc, h=s.h):
            ins = fn(t)
            if inc:
                ins.then_inc(h, 1)
        self.q["tensor"].append(run)
        return s.n

    def dma(self, eng, out, in_, sem=None, **kw):
        sem = sem or ("GD" if eng == "gpsimd" else "LD")
        s = self.s[sem]
        prev = s.n
        s.n += 16
        def run(e, out=out, in_=in_, h=s.h, kw=kw, prev=prev):
            if prev:
                e.wait_ge(h, prev)
            e.dma_start(out=out, in_=in_, **kw).then_inc(h, 16)
        self.q[eng].append(run)
        return s.n

    def idma(self, out, out_off, in_, in_off):
        s = self.s["GD"]
        prev = s.n
        s.n += 16
        def run(g, out=out, oo=out_off, in_=in_, io=in_off, h=s.h, prev=prev):
            if prev:
                g.wait_ge(h, prev)
            g.indirect_dma_start(out, oo, in_, io).then_inc(h, 16)
        self.q["gpsimd"].append(run)
        return s.n

    def coll(self, ins_ap, outs_ap):
        s = self.s["CC"]
        s.n += 1
        def run(g, i=ins_ap, o=outs_ap, h=s.h):
            g.collective_compute("AllGather", ALU.bypass,
                                 replica_groups=[list(range(8))],
                                 ins=[i], outs=[o]).then_inc(h, 1)
        self.q["gpsimd"].append(run)
        return s.n


def build_nc(dbg=()):
    from contextlib import ExitStack
    nc = bass.Bass("TRN2", target_bir_lowering=False)
    ctx = ExitStack()
    b = Build(nc, ctx, dbg)
    dt = nc.dram_tensor

    x_in = dt("x_in", [T, D], F32, kind="ExternalInput").ap()
    out_o = dt("out_o", [T, D], F32, kind="ExternalOutput").ap()
    wq_in = dt("wq_in", [D, D], F32, kind="ExternalInput").ap()
    wkv_in = dt("wkv_in", [D, 512], F32, kind="ExternalInput").ap()
    wout_in = dt("wout_in", [D, D], F32, kind="ExternalInput").ap()
    wqkv_in = dt("wqkv_in", [D, 3072], F32, kind="ExternalInput").ap()
    wproj_in = dt("wproj_in", [D, D], F32, kind="ExternalInput").ap()
    wgate_in = dt("wgate_in", [D, E], F32, kind="ExternalInput").ap()
    w1_in = dt("w1_in", [D, 4096], F32, kind="ExternalInput").ap()
    w2_in = dt("w2_in", [4096, D], F32, kind="ExternalInput").ap()
    lnA_g = dt("lnA_g", [128, D], F32, kind="ExternalInput").ap()
    lnA_b = dt("lnA_b", [128, D], F32, kind="ExternalInput").ap()
    ln1_g = dt("ln1_g", [128, D], F32, kind="ExternalInput").ap()
    ln1_b = dt("ln1_b", [128, D], F32, kind="ExternalInput").ap()
    ln2_g = dt("ln2_g", [128, D], F32, kind="ExternalInput").ap()
    ln2_b = dt("ln2_b", [128, D], F32, kind="ExternalInput").ap()
    qn_s = dt("qn_s", [128, D], F32, kind="ExternalInput").ap()
    qn_b = dt("qn_b", [128, D], F32, kind="ExternalInput").ap()
    kn_s = dt("kn_s", [128, 256], F32, kind="ExternalInput").ap()
    kn_b = dt("kn_b", [128, 256], F32, kind="ExternalInput").ap()
    b1_in = dt("b1_in", [128, 32], F32, kind="ExternalInput").ap()
    b2_in = dt("b2_in", [128, D], F32, kind="ExternalInput").ap()
    ident_in = dt("ident_in", [128, 128], BF16, kind="ExternalInput").ap()
    ltri_in = dt("ltri_in", [128, 128], BF16, kind="ExternalInput").ap()
    ones_in = dt("ones_in", [128, 128], BF16, kind="ExternalInput").ap()
    cmask_in = dt("cmask_in", [128, 128], BF16, kind="ExternalInput").ap()
    rCq_in = dt("rCq_in", [128, NT, 32], BF16, kind="ExternalInput").ap()
    rSq_in = dt("rSq_in", [128, NT, 32], BF16, kind="ExternalInput").ap()
    rCk_in = dt("rCk_in", [128, NT, 32], BF16, kind="ExternalInput").ap()
    rSk_in = dt("rSk_in", [128, NT, 32], BF16, kind="ExternalInput").ap()
    iota8_in = dt("iota8_in", [128, 8], F32, kind="ExternalInput").ap()
    esel_in = dt("esel_in", [128, 8], F32, kind="ExternalInput").ap()
    ownB_in = dt("ownB_in", [128, 8], F32, kind="ExternalInput").ap()
    iotarow_in = dt("iotarow_in", [128, 128], F32, kind="ExternalInput").ap()
    iotaslot_in = dt("iotaslot_in", [128, 32], F32, kind="ExternalInput").ap()
    pcol_in = dt("pcol_in", [128, 1], BF16, kind="ExternalInput").ap()

    d_x1 = dt("d_x1", [T, D], F32).ap()
    d_y = dt("d_y", [T, D], F32).ap()
    d_x2 = dt("d_x2", [T, D], F32).ap()
    d_h2 = dt("d_h2", [T, D], BF16).ap()
    d_ep1 = dt("d_ep1", [16, 128], BF16).ap()
    d_ep2 = dt("d_ep2", [16, 128], BF16).ap()
    ag_e1 = dt("ag_e1", [8, 16, 128], BF16).ap()
    ag_e2 = dt("ag_e2", [8, 16, 128], BF16).ap()
    ag_h2 = dt("ag_h2", [4, 8, 512, D], BF16).ap()
    d_prow = dt("d_prow", [128, 128], F32).ap()
    d_exp = dt("d_exp", [CAP, D], BF16).ap()
    ag_exp = dt("ag_exp", [4, E, 1024, D], BF16).ap()

    with ctx:
        A = nc.alloc_sbuf_tensor_at
        hf = A("hf", [128, FC, T], BF16, offset=OA)
        q1f = A("q1f", [128, FC, T], BF16, offset=OB)
        wq_sb = A("wq_sb", [128, FC, D], BF16, offset=OC)
        wkv_sb = A("wkv_sb", [128, FC, 512], BF16, offset=OC + 16 * KB)
        wqkvh0 = A("wqkvh0", [128, FC, 192], BF16, offset=OC)
        wqkvh1 = A("wqkvh1", [128, FC, 192], BF16, offset=OC + 3 * KB)
        qkvt = A("qkvt", [128, NT, 192], BF16, offset=OC + 8 * KB)
        q2r = A("q2r", [128, NT, HD], BF16, offset=OC + 14 * KB)
        k2r = A("k2r", [128, NT, HD], BF16, offset=OC + 16 * KB)
        q2f = A("q2f", [64, T], BF16, offset=OC + 18 * KB)
        k2f = A("k2f", [64, T], BF16, offset=OC + 22 * KB)
        rs1 = A("rs1", [128, NT, 32], BF16, offset=OC + 26 * KB)
        rs2 = A("rs2", [128, NT, 32], BF16, offset=OC + 27 * KB)
        rs3 = A("rs3", [128, NT, 32], BF16, offset=OC + 28 * KB)
        rs4 = A("rs4", [128, NT, 32], BF16, offset=OC + 29 * KB)
        wproj_sb = A("wproj_sb", [128, FC, D], BF16, offset=OC + 6 * KB)
        k1f = A("k1f", [128, 2, T], BF16, offset=OD)
        v1t = A("v1t", [128, NT, 256], BF16, offset=OD + 8 * KB)
        wout_sb = A("wout_sb", [128, FC, D], BF16, offset=OD + 16 * KB)
        e_xt = A("e_xt", [128, D], F32, offset=OE)
        e_c1 = A("e_c1", [128, D], F32, offset=OE + 4 * KB)
        e_c2 = A("e_c2", [128, D], F32, offset=OE + 8 * KB)
        e_c3 = A("e_c3", [128, D], F32, offset=OE + 12 * KB)
        e_b1 = A("e_b1", [128, D], BF16, offset=OE + 16 * KB)
        e_b2 = A("e_b2", [128, D], BF16, offset=OE + 18 * KB)
        e_s4 = A("e_s4", [128, NT, 8], F32, offset=OE + 20 * KB)
        bn6 = A("bn6", [128, 2, 6], F32, offset=OH + 10 * KB + 512)
        bn2 = A("bn2", [128, 2], F32, offset=OH + 10 * KB + 576)
        e_s5 = A("e_s5", [128, D], F32, offset=OF + 24 * KB)
        oall = A("oall", [128, NT, D], BF16, offset=OF)
        h2f = A("h2f", [128, FC, T], BF16, offset=OB + 16 * KB)
        g_v0 = A("g_v0", [128, D], F32, offset=OG)
        g_v1 = A("g_v1", [128, D], F32, offset=OG + 4 * KB)
        g_kn0 = A("g_kn0", [128, 256], F32, offset=OF + 8 * KB)
        g_kn1 = A("g_kn1", [128, 256], F32, offset=OF + 9 * KB)
        PT2 = A("PT2", [128, NT, 128], BF16, offset=OE + 25 * KB)
        b2sb = A("b2sb", [128, D], F32, offset=OE + 12 * KB)
        ident = A("ident", [128, 128], BF16, offset=OH)
        ltri = A("ltri", [128, 128], BF16, offset=OH + 256)
        ones = A("ones", [128, 128], BF16, offset=OH + 512)
        cmask = A("cmask", [128, 128], BF16, offset=OH + 768)
        rCq = A("rCq", [128, NT, 32], BF16, offset=OH + 1 * KB)
        rSq = A("rSq", [128, NT, 32], BF16, offset=OH + 2 * KB)
        rCk = A("rCk", [128, NT, 32], BF16, offset=OH + 3 * KB)
        rSk = A("rSk", [128, NT, 32], BF16, offset=OH + 4 * KB)
        iota8 = A("iota8", [128, 8], F32, offset=OH + 5 * KB)
        esel = A("esel", [128, 8], F32, offset=OH + 5 * KB + 32)
        ownB = A("ownB", [128, 8], F32, offset=OH + 5 * KB + 64)
        epsv = A("epsv", [128, 2], F32, offset=OH + 5 * KB + 96)
        iotarow = A("iotarow", [128, 128], F32, offset=OH + 5 * KB + 512)
        iotaslot = A("iotaslot", [128, 32], F32, offset=OH + 6 * KB)
        pcol = A("pcol", [128, 1], BF16, offset=OH + 6 * KB + 128)
        b1sb = A("b1sb", [128, 32], F32, offset=OH + 6 * KB + 256)
        wg_sb = A("wg_sb", [128, FC, 8], BF16, offset=OH + 6 * KB + 384)
        stat0 = A("stat0", [128, 16], F32, offset=OH + 6 * KB + 512)
        stat1 = A("stat1", [128, 16], F32, offset=OH + 6 * KB + 576)
        stat2 = A("stat2", [128, 16], F32, offset=OH + 6 * KB + 640)
        stat3 = A("stat3", [128, 16], F32, offset=OH + 6 * KB + 704)
        negmax = A("negmax", [128, 1], F32, offset=OH + 6 * KB + 768)
        sums = A("sums", [128, 1], F32, offset=OH + 6 * KB + 800)
        recs = A("recs", [128, 1], F32, offset=OH + 6 * KB + 832)
        lgt = A("lgt", [128, NT, 8], F32, offset=OH + 7 * KB)
        gts = A("gts", [128, NT, 8], F32, offset=OH + 7 * KB + 512)
        t2a = A("t2a", [128, NT, 8], F32, offset=OH + 8 * KB)
        t2b = A("t2b", [128, NT, 8], F32, offset=OH + 8 * KB + 512)
        e1t = A("e1t", [128, NT], F32, offset=OH + 9 * KB)
        e2t = A("e2t", [128, NT], F32, offset=OH + 9 * KB + 64)
        g1t = A("g1t", [128, NT], F32, offset=OH + 9 * KB + 128)
        g2t = A("g2t", [128, NT], F32, offset=OH + 9 * KB + 192)
        pos1 = A("pos1", [128, NT], F32, offset=OH + 9 * KB + 256)
        pos2 = A("pos2", [128, NT], F32, offset=OH + 9 * KB + 320)
        sc1 = A("sc1", [128, NT], F32, offset=OH + 9 * KB + 384)
        sc2 = A("sc2", [128, NT], F32, offset=OH + 9 * KB + 448)
        idx1 = A("idx1", [128, NT], I32, offset=OH + 9 * KB + 512)
        idx2 = A("idx2", [128, NT], I32, offset=OH + 9 * KB + 576)
        tokl = A("tokl", [128, 32], I32, offset=OH + 9 * KB + 640)
        prowg = A("prowg", [128, 32], F32, offset=OH + 9 * KB + 768)
        fidx = A("fidx", [128, 32], I32, offset=OH + 9 * KB + 896)
        kp1 = A("kp1", [128, NT], F32, offset=OH + 10 * KB)
        kp2 = A("kp2", [128, NT], F32, offset=OH + 10 * KB + 64)
        P_sb = A("P_sb", [128, T], BF16, offset=OE + 21 * KB)
        P_sb2 = A("P_sb2", [128, T], BF16, offset=OE)
        qkvtB = A("qkvtB", [128, NT, 192], BF16, offset=OB)
        q2fB = A("q2fB", [64, T], BF16, offset=OB + 6 * KB)
        k2fB = A("k2fB", [64, T], BF16, offset=OB + 10 * KB)
        rtmp = A("rtmp", [128, 32, 32], BF16, offset=OB + 14 * KB)
        PT2b = A("PT2b", [128, NT, 128], BF16, offset=OE + 4 * KB)
        negmax2 = A("negmax2", [128, 1], F32, offset=OH + 6 * KB + 864)
        sums2 = A("sums2", [128, 1], F32, offset=OH + 6 * KB + 896)
        recs2 = A("recs2", [128, 1], F32, offset=OH + 6 * KB + 928)
        smp4a = A("smp4a", [128, 4], F32, offset=OH + 6 * KB + 960)
        sumsA = A("sumsA", [128, NT, 16], F32, offset=OC)
        rsA = A("rsA", [128, NT, 16], F32, offset=OC + 1 * KB)
        g45a = A("g45a", [128, D], F32, offset=OC + 2 * KB)
        g45b = A("g45b", [128, D], F32, offset=OC + 6 * KB)
        s45 = A("s45", [128, D], F32, offset=OC + 10 * KB)
        sumsB = A("sumsB", [128, NT, 16], F32, offset=OE + 9 * KB)
        rsB = A("rsB", [128, NT, 16], F32, offset=OE + 11 * KB)
        smp4b = A("smp4b", [128, 4], F32, offset=OH + 6 * KB + 992)
        e1g = A("e1g", [128, 128], BF16, offset=OE)
        e2g = A("e2g", [128, 128], BF16, offset=OE + 256)
        mgs1 = A("mgs1", [128, 128], F32, offset=OE + 512)
        maskf = A("maskf", [128, E, 128], F32, offset=OE + 1 * KB)
        maskb = A("maskb", [128, E, 128], BF16, offset=OE + 5 * KB)
        mgs2 = A("mgs2", [128, 128], F32, offset=OE + 7 * KB + 512)
        ppref = A("ppref", [128, E, 128], F32, offset=OE + 8 * KB)
        colsum = A("colsum", [128, E, 128], F32, offset=OE + 12 * KB)
        csh1 = A("csh1", [128, E, 128], F32, offset=OE + 16 * KB)
        csh2 = A("csh2", [128, E, 128], F32, offset=OE + 20 * KB)
        posg = A("posg", [128, E, 128], F32, offset=OE + 24 * KB)
        ownps = A("ownps", [128, E, NT], F32, offset=OE + 28 * KB)
        myppref = A("myppref", [128, 128], F32, offset=OE + 28 * KB + 512)
        mycolpre = A("mycolpre", [128, 128], F32, offset=OE + 29 * KB)
        mymask = A("mymask", [128, 128], F32, offset=OE + 29 * KB + 512)
        ohf = A("ohf", [128, 128], F32, offset=OE + 30 * KB)
        ohbf = A("ohbf", [128, 128], BF16, offset=OE + 30 * KB + 512)
        prow_sb = A("prow_sb", [128, 128], F32, offset=OE + 31 * KB)
        cmpb = A("cmpb", [128, 32, 128], F32, offset=OF)
        srA = A("srA", [128, 32, 128], F32, offset=OF + 16 * KB)
        colS = A("colS", [128, 32], F32, offset=OH + 10 * KB + 128)
        cbS = A("cbS", [128, 32], F32, offset=OH + 10 * KB + 256)
        rkS = A("rkS", [128, 32], F32, offset=OH + 10 * KB + 384)
        w2sb = A("w2sb", [128, 32, D], BF16, offset=OA)
        w1sb = A("w1sb", [128, FC, 4096], BF16, offset=OC)
        xg = A("xg", [128, D], BF16, offset=OE)
        xf = A("xf", [128, FC, 128], BF16, offset=OE + 2 * KB)
        hmb = A("hmb", [128, FC, 128], BF16, offset=OE + 4 * KB)
        hsc = A("hsc", [128, D], F32, offset=OE + 6 * KB)
        outt = A("outt", [128, D], BF16, offset=OE + 10 * KB)
        xg2 = A("xg2", [128, D], BF16, offset=OE + 25 * KB)
        hsc2 = A("hsc2", [128, D], F32, offset=OE + 29 * KB)
        hmb2 = A("hmb2", [128, FC, 128], BF16, offset=OF + 20 * KB)
        xf2 = A("xf2", [128, FC, 128], BF16, offset=OE + 27 * KB)
        cA1 = A("cA1", [128, D], BF16, offset=OE + 16 * KB)
        cA2 = A("cA2", [128, D], BF16, offset=OE + 18 * KB)
        ct1 = A("ct1", [128, D], F32, offset=OF)
        ct2 = A("ct2", [128, D], F32, offset=OF + 4 * KB)
        cx1 = A("cx1", [128, D], F32, offset=OF + 8 * KB)
        cx2 = A("cx2", [128, D], F32, offset=OF + 12 * KB)
        co = A("co", [128, D], F32, offset=OF + 16 * KB)
        cA1b = A("cA1b", [128, D], BF16, offset=OE + 21 * KB)
        cA2b = A("cA2b", [128, D], BF16, offset=OE + 23 * KB)
        cx1b = A("cx1b", [128, D], F32, offset=OF + 20 * KB)
        cx2b = A("cx2b", [128, D], F32, offset=OF + 24 * KB)
        cob = A("cob", [128, D], F32, offset=OF + 28 * KB)

        PS = nc.alloc_psum_tensor("PS", [128, 4096], F32)
        psS = PS[:, 0:2048]
        psSr = [PS[:, 0:2048], PS[:, 2048:4096]]
        psPTr = [PS[:, 0:2048].bitcast(BF16), PS[:, 2048:4096].bitcast(BF16)]
        psOr = [PS[:, 384:448], PS[:, 2432:2496]]
        psXa = PS[:, 2048:2240]
        psXb = PS[:, 2560:2752]
        psXp = [PS[:, 768:960], PS[:, 2816:3008]]
        psTq = [PS[0:64, 1280:1344].bitcast(BF16), PS[0:64, 3328:3392].bitcast(BF16)]
        psTk = [PS[0:64, 1344:1408].bitcast(BF16), PS[0:64, 3392:3456].bitcast(BF16)]
        psX = PS[:, 2048:3072]
        psPT = PS[:, 2048:2560].bitcast(BF16)
        psO = PS[:, 3072:3136]
        psL = PS[:, 3136:3144]
        psT = PS[:, 3584:4096].bitcast(BF16)
        psPP = PS[:, 0:1024]
        psCS = PS[:, 1024:2048]
        psPR = PS[:, 2560:3072]
        psHm = PS[:, 0:1024]
        psOm = PS[:, 1024:2048]

        S = b.s
        LD, GD, CC, TE, VE, SE, GE, ST = (S[k] for k in
                                          ["LD", "GD", "CC", "TE", "VE", "SE", "GE", "ST"])

        def dbg_dump(name, src, shape, dtyp, waits):
            if name not in b.dbg:
                return
            o = dt("dbg_" + name, shape, dtyp, kind="ExternalOutput").ap()
            for sem, val in waits:
                b.wait("sync", sem, val)
            b.dma("sync", o, src, sem="ST")

        for dst_t, src in [(ident, ident_in), (ltri, ltri_in), (ones, ones_in),
                           (cmask, cmask_in), (rCq, rCq_in), (rSq, rSq_in),
                           (rCk, rCk_in), (rSk, rSk_in), (iota8, iota8_in),
                           (esel, esel_in), (ownB, ownB_in), (iotarow, iotarow_in),
                           (iotaslot, iotaslot_in), (pcol, pcol_in), (b1sb, b1_in)]:
            b.dma("sync", dst_t[:], src)
        b.dma("sync", g_v0[:], lnA_g)
        b.dma("sync", g_v1[:], lnA_b)
        ld0 = LD.n
        b.wait("vector", "LD", ld0)
        b.vdo(lambda v: v.tensor_scalar(epsv[:, 0:1], ones[:, 0:1], 0.0, 1e-5, ALU.mult, ALU.add))
        b.vdo(lambda v: v.tensor_scalar(epsv[:, 1:2], ones[:, 0:1], 0.0, 1e-6, ALU.mult, ALU.add), inc=True)
        b.dma("gpsimd", wq_sb[:], wq_in.rearrange("(j p) o -> p j o", p=128))
        b.dma("gpsimd", wkv_sb[:], wkv_in.rearrange("(j p) o -> p j o", p=128))
        b.dma("gpsimd", wg_sb[:], wgate_in.rearrange("(j p) o -> p j o", p=128))
        gd0 = GD.n

        def ln_plain_tile(src_sb, dst_sb, eps, sA, sB, tmp):
            ev = epsv[:, 0:1] if eps == 1e-5 else epsv[:, 1:2]
            b.vdo(lambda v: (v.bn_stats(bn6[:, 0, :], src_sb[:, 0:512]),
                             v.bn_stats(bn6[:, 1, :], src_sb[:, 512:1024]))[-1], drain=True)
            b.vdo(lambda v: v.bn_aggr(bn2[:], bn6[:]), inc=True, drain=True)
            b.wait("scalar", "VE", b.s["VE"].n)
            sv = b.sdo(lambda s: s.activation(out=sB[:, 2:3], in_=bn2[:, 1:2], func=AF.Sqrt,
                                              bias=ev, scale=1.0), inc=True)
            b.wait("vector", "SE", sv)
            b.vdo(lambda v: v.reciprocal(sA[:, 2:3], sB[:, 2:3]), drain=True)
            b.vdo(lambda v: v.scalar_tensor_tensor(dst_sb[:], src_sb[:], bn2[:, 0:1], 
                                                   sA[:, 2:3].broadcast_to([128, D]),
                                                   ALU.subtract, ALU.mult), drain=False)

        def ln_gb_tile(src_sb, dst_sb, eps, gm, bt, tmp, s5=None):
            ev = epsv[:, 0:1] if eps == 1e-5 else epsv[:, 1:2]
            b.vdo(lambda v: (v.bn_stats(bn6[:, 0, :], src_sb[:, 0:512]),
                             v.bn_stats(bn6[:, 1, :], src_sb[:, 512:1024]))[-1], drain=True)
            b.vdo(lambda v: v.bn_aggr(bn2[:], bn6[:]), inc=True, drain=True)
            b.wait("scalar", "VE", b.s["VE"].n)
            sv = b.sdo(lambda s: s.activation(out=stat1[:, 6:7], in_=bn2[:, 1:2], func=AF.Sqrt,
                                              bias=ev, scale=1.0), inc=True)
            b.wait("vector", "SE", sv)
            b.vdo(lambda v: v.reciprocal(stat0[:, 6:7], stat1[:, 6:7]), drain=True)
            b.vdo(lambda v: v.scalar_tensor_tensor(tmp[:], src_sb[:], bn2[:, 0:1],
                                                   stat0[:, 6:7].broadcast_to([128, D]),
                                                   ALU.subtract, ALU.mult), drain=True)
            s5 = e_s5 if s5 is None else s5
            b.vdo(lambda v: v.tensor_mul(s5[:], tmp[:], gm[:]), drain=True)
            b.vdo(lambda v: v.tensor_add(dst_sb[:], s5[:], bt[:]), drain=False)

        def to_featmajor(i, src_bf, dst_ff, nch=FC):
            def tp(t, i=i):
                for j in range(nch):
                    ins = t.transpose(psT[:, 128 * j:128 * (j + 1)],
                                      src_bf[:, 128 * j:128 * (j + 1)], ident[:])
                return ins
            tv = b.tdo(tp)
            b.wait("scalar", "TE", tv)
            se = b.sdo(lambda s, i=i: s.copy(dst_ff[:, :, 128 * i:128 * (i + 1)],
                                             psT[:, 0:128 * nch].rearrange("p (c q) -> p c q", c=nch)),
                       inc=True, drain=False)
            b.wait("tensor", "SE", se)
            b.wait("vector", "SE", se)
            return se

        b.wait("vector", "LD", ld0)
        for i in range(NT):
            if i > 0:
                b.wait("sync", "VE", VE.n)
            b.dma("sync", e_xt[:], x_in[128 * i:128 * (i + 1), :])
            b.wait("vector", "LD", LD.n)
            ln_plain_tile(e_xt, e_c2, 1e-5, stat0, stat1, e_c1)
            ln_gb_tile(e_c2, e_c3, 1e-6, g_v0, g_v1, e_c1)
            b.vdo(lambda v: v.tensor_copy(e_b1[:], e_c3[:]), inc=True)
            b.wait("tensor", "VE", VE.n)
            to_featmajor(i, e_b1, hf)
        ph1_se = SE.n
        dbg_dump("xnf", hf[:], [128, FC, T], BF16, [("SE", ph1_se)])

        b.wait("tensor", "GD", gd0)
        b.wait("sync", "SE", ph1_se)
        b.dma("sync", e_c2[:], qn_s)
        b.dma("sync", e_c3[:], qn_b)
        b.wait("vector", "LD", LD.n)
        b.vdo(lambda v: v.tensor_scalar_mul(g_v0[:], e_c2[:], 0.125))
        b.vdo(lambda v: v.tensor_scalar_mul(g_v1[:], e_c3[:], 0.125), inc=True)
        vqn = VE.n
        b.wait("sync", "VE", vqn)
        b.dma("sync", g_kn0[:], kn_s)
        b.dma("sync", g_kn1[:], kn_b)
        ld_kn = LD.n
        b.wait("vector", "LD", ld_kn)

        def headln(tile_f32, n_grp, gm, bt, eps):
            w = n_grp * HD
            src3 = tile_f32[:, 0:w].rearrange("p (a c) -> p a c", a=n_grp)
            b.vdo(lambda v: v.reduce_sum(stat0[:, 0:n_grp], src3, axis=AX.X))
            b.vdo(lambda v: v.tensor_scalar_mul(stat1[:, 0:n_grp], stat0[:, 0:n_grp], 1.0 / HD))
            b.vdo(lambda v: v.tensor_sub(e_c1[:, 0:w].rearrange("p (a c) -> p a c", a=n_grp), src3,
                                         stat1[:, 0:n_grp].unsqueeze(2).broadcast_to([128, n_grp, HD])))
            b.vdo(lambda v: v.tensor_mul(e_c2[:, 0:w], e_c1[:, 0:w], e_c1[:, 0:w]))
            b.vdo(lambda v: v.reduce_sum(stat2[:, 0:n_grp],
                                         e_c2[:, 0:w].rearrange("p (a c) -> p a c", a=n_grp), axis=AX.X))
            b.vdo(lambda v: v.tensor_scalar(stat3[:, 0:n_grp], stat2[:, 0:n_grp], 1.0 / HD, eps,
                                            ALU.mult, ALU.add))
            b.vdo(lambda v: v.reciprocal(stat0[:, 0:n_grp], stat3[:, 0:n_grp]), inc=True)
            b.wait("scalar", "VE", b.s["VE"].n)
            sv = b.sdo(lambda s: s.activation(out=stat2[:, 0:n_grp], in_=stat0[:, 0:n_grp],
                                              func=AF.Sqrt), inc=True)
            b.wait("vector", "SE", sv)
            b.vdo(lambda v: v.tensor_mul(e_c2[:, 0:w].rearrange("p (a c) -> p a c", a=n_grp),
                                         e_c1[:, 0:w].rearrange("p (a c) -> p a c", a=n_grp),
                                         stat2[:, 0:n_grp].unsqueeze(2).broadcast_to([128, n_grp, HD])))
            b.vdo(lambda v: v.tensor_mul(e_c1[:, 0:w], e_c2[:, 0:w], gm))
            b.vdo(lambda v: v.tensor_add(e_c3[:, 0:w], e_c1[:, 0:w], bt))

        for i in range(NT):
            def qmm(t, i=i):
                for j in range(FC):
                    for h2 in range(2):
                        ins = t.matmul(psX[:, 512 * h2:512 * (h2 + 1)],
                                       hf[:, j, 128 * i:128 * (i + 1)],
                                       wq_sb[:, j, 512 * h2:512 * (h2 + 1)],
                                       start=(j == 0), stop=(j == FC - 1))
                return ins
            tq = b.tdo(qmm)
            b.wait("vector", "TE", tq)
            b.vdo(lambda v: v.tensor_copy(e_xt[:], psX))
            headln(e_xt, H, g_v0[:], g_v1[:], 1e-6)
            b.vdo(lambda v: v.tensor_copy(e_b1[:], e_c3[:]), inc=True)
            b.wait("tensor", "VE", VE.n)
            to_featmajor(i, e_b1, q1f)
            def kvmm(t, i=i):
                for j in range(FC):
                    ins = t.matmul(psX[:, 0:512], hf[:, j, 128 * i:128 * (i + 1)],
                                   wkv_sb[:, j, :], start=(j == 0), stop=(j == FC - 1))
                return ins
            tk = b.tdo(kvmm)
            b.wait("vector", "TE", tk)
            b.vdo(lambda v: v.tensor_copy(e_xt[:, 0:256], psX[:, 0:256]))
            vv = b.vdo(lambda v, i=i: v.tensor_copy(v1t[:, i, :], psX[:, 256:512]), inc=True)
            headln(e_xt, G4, g_kn0[:], g_kn1[:], 1e-6)
            b.vdo(lambda v: v.tensor_copy(e_b1[:, 0:256], e_c3[:, 0:256]), inc=True)
            b.wait("tensor", "VE", VE.n)
            def ktp(t, i=i):
                for j in range(2):
                    ins = t.transpose(psT[:, 128 * j:128 * (j + 1)],
                                      e_b1[:, 128 * j:128 * (j + 1)], ident[:])
                return ins
            tk2 = b.tdo(ktp)
            b.wait("scalar", "TE", tk2)
            sk = b.sdo(lambda s, i=i: s.copy(k1f[:, :, 128 * i:128 * (i + 1)],
                                             psT[:, 0:256].rearrange("p (c q) -> p c q", c=2)), inc=True)
            b.wait("tensor", "SE", sk)
            b.wait("vector", "SE", sk)
        ph2_se, ph2_ve = SE.n, VE.n
        dbg_dump("q1f", q1f[:], [128, FC, T], BF16, [("SE", ph2_se)])
        dbg_dump("k1f", k1f[:], [128, 2, T], BF16, [("SE", ph2_se)])
        dbg_dump("v1t", v1t[:], [128, NT, 256], BF16, [("VE", ph2_ve)])

        b.wait("gpsimd", "SE", ph2_se)
        b.dma("gpsimd", wout_sb[:], wout_in.rearrange("(j p) o -> p j o", p=128))
        gd_wout = GD.n

        b.wait("tensor", "SE", ph2_se)
        b.wait("tensor", "VE", ph2_ve)

        Pp = [P_sb, P_sb2]
        PTp = [PT2, PT2b]
        nmp = [negmax, negmax2]
        smp = [sums, sums2]
        rcp = [recs, recs2]

        smp4 = [smp4a, smp4b]

        def attn_pipe(iters, masked, inject_t=None, inject_v=None, inject_s=None):
            N = len(iters)
            tS = [0] * N
            tPV = [0] * N
            vMax = [0] * N
            vRc = [0] * N
            vOut = [0] * N
            injV = [0] * N
            injS = [0] * N
            sExp4 = [[0] * 4 for _ in range(N)]

            def groups(nk):
                return [(4 * i, min(4, nk - 4 * i)) for i in range((nk + 3) // 4)]

            def emit_S(n):
                it = iters[n]
                r = n % 2
                nk = it["nk"]
                kw = nk * 128
                if n >= 2:
                    b.wait("tensor", "SE", vOut[n - 2])
                    if injV[n - 2]:
                        b.wait("tensor", "VE", injV[n - 2])
                    if injS[n - 2]:
                        b.wait("tensor", "SE", injS[n - 2])
                sw = it.get("sw")
                if sw is not None:
                    swv = sw()
                    if swv:
                        b.wait("tensor", "SE", swv)
                def smm(t, it=it, r=r, nk=nk, kw=kw, masked=masked):
                    nchunk = (kw + 511) // 512
                    dbank = ((nk - 1) * 128) // 512
                    ins = None
                    for kc in range(nchunk):
                        w = min(512, kw - 512 * kc)
                        ins = t.matmul(psSr[r][:, 512 * kc:512 * kc + w], it["lq"],
                                       it["rk"](512 * kc, w),
                                       start=True, stop=(not masked) or (kc != dbank))
                    if masked:
                        ins = t.matmul(psSr[r][:, 128 * (nk - 1):128 * nk], ident[:],
                                       cmask[:], start=False, stop=True)
                    return ins
                tS[n] = b.tdo(smm)

            def emit_vmax(n):
                r = n % 2
                kw = iters[n]["nk"] * 128
                b.wait("vector", "TE", tS[n])
                vMax[n] = b.vdo(lambda v, r=r, kw=kw: v.reduce_max(
                    nmp[r][:], psSr[r][:, 0:kw].rearrange("p (a c) -> p a c", a=1),
                    axis=AX.X, negate=True), inc=True, drain=False)

            def emit_exp(n):
                r = n % 2
                it = iters[n]
                grs = groups(it["nk"])
                ng = len(grs)
                b.wait("scalar", "VE", vMax[n])
                for gi, (st, sz) in enumerate(grs):
                    acc = it["sacc"] if ng == 1 else smp4[r][:, gi:gi + 1]
                    sExp4[n][gi] = b.sdo(lambda sc, r=r, st=st, sz=sz, acc=acc: sc.activation(
                        out=Pp[r][:, 128 * st:128 * (st + sz)],
                        in_=psSr[r][:, 128 * st:128 * (st + sz)], func=AF.Exp,
                        bias=nmp[r][:], scale=1.0,
                        accum_out=acc), inc=True, drain=False)

            def emit_vsum(m):
                it = iters[m]
                r = m % 2
                ng = len(groups(it["nk"]))
                if ng >= 2:
                    b.wait("vector", "SE", sExp4[m][ng - 1])
                    b.vdo(lambda v, r=r, ng=ng, it=it: v.reduce_sum(
                        it["sacc"], smp4[r][:, 0:ng].rearrange("p (a c) -> p a c", a=1),
                        axis=AX.X), inc=True, drain=False)

            def emit_tail(m):
                it = iters[m]
                r = m % 2
                nk = it["nk"]
                grs = groups(nk)
                ng = len(grs)
                ev4 = []
                for gi, (st, sz) in enumerate(grs):
                    b.wait("tensor", "SE", sExp4[m][gi])
                    def ptp(t, r=r, gi=gi, sz=sz):
                        for i in range(sz):
                            ins = t.transpose(
                                psPTr[r][:, 1024 * gi + 128 * i:1024 * gi + 128 * (i + 1)],
                                Pp[r][:, 128 * (4 * gi + i):128 * (4 * gi + i + 1)], ident[:])
                        return ins
                    tT = b.tdo(ptp)
                    if gi % 2 == 0:
                        b.wait("vector", "TE", tT)
                        c = b.vdo(lambda v, r=r, gi=gi, st=st, sz=sz: v.tensor_copy(
                            PTp[r][:, st:st + sz, :],
                            psPTr[r][:, 1024 * gi:1024 * gi + 128 * sz].rearrange(
                                "p (a c) -> p a c", a=sz)), inc=True, drain=False)
                        ev4.append(("VE", c))
                    else:
                        b.wait("scalar", "TE", tT)
                        c = b.sdo(lambda sc, r=r, gi=gi, st=st, sz=sz: sc.copy(
                            PTp[r][:, st:st + sz, :],
                            psPTr[r][:, 1024 * gi:1024 * gi + 128 * sz].rearrange(
                                "p (a c) -> p a c", a=sz)), inc=True, drain=False)
                        ev4.append(("SE", c))
                for gi, (st, sz) in enumerate(grs):
                    sem, c = ev4[gi]
                    b.wait("tensor", sem, c)
                    def pv(t, it=it, r=r, st=st, sz=sz, gi=gi, ng=ng):
                        for kt in range(st, st + sz):
                            ins = t.matmul(psOr[r], PTp[r][:, kt, :], it["vf"](kt),
                                           start=(kt == 0), stop=(kt == nk - 1))
                        return ins
                    tPV[m] = b.tdo(pv)
                b.wait("scalar", "TE", tPV[m])
                vOut[m] = b.sdo(lambda sc, it=it, r=r: it["out"](sc, psOr[r]), inc=True, drain=False)
                if inject_t is not None:
                    inject_t(m, sExp4[m])

            for n in range(N):
                emit_S(n)
                if n >= 1:
                    emit_tail(n - 1)
                emit_vmax(n)
                emit_exp(n)
                if n >= 1:
                    emit_vsum(n - 1)
                    if inject_v is not None:
                        injV[n - 1] = inject_v(n - 1)
                    if inject_s is not None:
                        injS[n - 1] = inject_s(n - 1)
            emit_tail(N - 1)
            emit_vsum(N - 1)
            if inject_v is not None:
                injV[N - 1] = inject_v(N - 1)
            if inject_s is not None:
                injS[N - 1] = inject_s(N - 1)
            b.wait("tensor", "SE", vOut[N - 1])
            b.wait("vector", "SE", vOut[N - 1])
            b.wait("gpsimd", "SE", vOut[N - 1])
            b.wait("sync", "SE", vOut[N - 1])

        QA = [0, 1, 2, 3, 8, 9, 10, 11]
        QB = [4, 5, 6, 7, 12, 13, 14, 15]
        def qloc(h):
            return (QA.index(h), 0) if h in QA else (QB.index(h), 1)
        iters1 = []
        for h in range(H):
            g = h // 4
            qc, qh = qloc(h)
            for qi in range(NT):
                def mkout(h=h, qi=qi):
                    return lambda sc, pso: sc.copy(oall[:, qi, 64 * h:64 * (h + 1)], pso)
                iters1.append(dict(
                    lq=q1f[64 * qh:64 * qh + 64, qc, 128 * qi:128 * (qi + 1)],
                    rk=(lambda off, w, g=g: k1f[64 * (g % 2):64 * (g % 2) + 64, g // 2, off:off + w]),
                    nk=NT,
                    vf=(lambda kt, g=g: v1t[:, kt, 64 * g:64 * (g + 1)]),
                    out=mkout(),
                    sacc=sumsA[:, qi, h:h + 1]))
        attn_pipe(iters1, masked=False)
        ph3_ve = VE.n
        dbg_dump("oall1", oall[:], [128, NT, D], BF16, [("VE", ph3_ve)])
        dbg_dump("sumsA", sumsA[:], [128, NT, 16], F32, [("VE", ph3_ve)])

        b.wait("tensor", "GD", gd_wout)
        of = hf
        b.wait("sync", "VE", VE.n)
        b.dma("sync", g45a[:], ln1_g)
        b.dma("sync", g45b[:], ln1_b)
        b.wait("vector", "LD", LD.n)
        b.vdo(lambda v: v.drain(), inc=False, drain=False)
        b.vdo(lambda v: v.reciprocal(rsA[:], sumsA[:]), inc=True, drain=True)
        psX2 = PS[:, 0:1024]
        veAdd45 = [0] * NT
        for i in range(NT):
            b.vdo(lambda v, i=i: v.tensor_mul(
                e_b1[:].rearrange("p (a c) -> p a c", a=H),
                oall[:, i, :].rearrange("p (a c) -> p a c", a=H),
                rsA[:, i, :].unsqueeze(2).broadcast_to([128, H, 64])), inc=True)
            b.wait("tensor", "VE", VE.n)
            tv_a = to_featmajor(i, e_b1, of)
            if i >= 1:
                b.wait("sync", "VE", veAdd45[i - 1])
            b.dma("sync", e_xt[:], x_in[128 * i:128 * (i + 1), :])
            ldx = LD.n
            px = psX if i % 2 == 0 else psX2
            if i >= 2:
                b.wait("tensor", "VE", veAdd45[i - 2])
            def pmm(t, i=i, px=px):
                for j in range(FC):
                    for h2 in range(2):
                        ins = t.matmul(px[:, 512 * h2:512 * (h2 + 1)],
                                       of[:, j, 128 * i:128 * (i + 1)],
                                       wout_sb[:, j, 512 * h2:512 * (h2 + 1)],
                                       start=(j == 0), stop=(j == FC - 1))
                return ins
            tm = b.tdo(pmm)
            b.wait("vector", "TE", tm)
            b.wait("vector", "LD", ldx)
            veAdd45[i] = b.vdo(lambda v, px=px: v.tensor_add(e_c1[:], px, e_xt[:]), inc=True)
            b.wait("sync", "VE", veAdd45[i])
            b.dma("sync", d_x1[128 * i:128 * (i + 1), :], e_c1[:])
            ld_x1 = LD.n
            ln_plain_tile(e_c1, e_c2, 1e-5, stat0, stat1, e_xt)
            b.vdo(lambda v: v.tensor_copy(e_c3[:], e_c2[:]), inc=True)
            b.wait("sync", "VE", VE.n)
            b.dma("sync", d_y[128 * i:128 * (i + 1), :], e_c3[:])
            b.wait("vector", "LD", LD.n)
            ln_gb_tile(e_c2, e_c3, 1e-5, g45a, g45b, e_c1, s5=s45)
            b.wait("vector", "TE", tv_a)
            b.vdo(lambda v: v.tensor_copy(e_b1[:], e_c3[:]), inc=True)
            b.wait("tensor", "VE", VE.n)
            to_featmajor(i, e_b1, hf)
            b.wait("vector", "LD", LD.n)
        ph4_ld = LD.n
        dbg_dump("x1", d_x1, [T, D], F32, [("LD", ph4_ld)])
        ph5_se, ph5_ld = SE.n, LD.n
        dbg_dump("y", d_y, [T, D], F32, [("LD", ph5_ld)])
        dbg_dump("hf2", hf[:], [128, FC, T], BF16, [("SE", ph5_se)])

        b.wait("gpsimd", "SE", ph5_se)
        qkvtP = [qkvt, qkvtB]
        q2fP = [q2f, q2fB]
        k2fP = [k2f, k2fB]
        wqkvP = [wqkvh0, wqkvh1]
        b.dma("gpsimd", wqkvh0[:],
              wqkv_in[:, 0:192].rearrange("(j p) o -> p j o", p=128))
        gdW = [0] * (H + 1)
        gdW[0] = GD.n
        b.dma("gpsimd", wqkvh1[:],
              wqkv_in[:, 192:384].rearrange("(j p) o -> p j o", p=128))
        gdW[1] = GD.n
        b.wait("tensor", "GD", gdW[0])
        scopyC = [[0] * NT for _ in range(H)]
        ropeC = [[0] * NT for _ in range(H)]
        projC15 = [0] * H
        tproj = [0] * NT
        vproj = [0] * NT
        for i in range(NT):
            px = psXa if i % 2 == 0 else psXb
            if i >= 2:
                b.wait("tensor", "VE", vproj[i - 2])
            def qkvmm(t, i=i, px=px):
                for j in range(FC):
                    ins = t.matmul(px, hf[:, j, 128 * i:128 * (i + 1)],
                                   wqkvh0[:, j, :], start=(j == 0), stop=(j == FC - 1))
                return ins
            tproj[i] = b.tdo(qkvmm)
            b.wait("vector", "TE", tproj[i])
            vproj[i] = b.vdo(lambda v, i=i, px=px: v.tensor_copy(qkvt[:, i, :], px), inc=True, drain=False)
        projC15[0] = tproj[NT - 1]
        qa, qb2 = qkvt[:, :, 0:32], qkvt[:, :, 32:64]
        ka, kb = qkvt[:, :, 64:96], qkvt[:, :, 96:128]
        b.vdo(lambda v: v.tensor_mul(rs1[:], qa, rCq[:]), drain=False)
        b.vdo(lambda v: v.tensor_mul(rs2[:], qb2, rSq[:]), drain=True)
        b.vdo(lambda v: v.tensor_sub(q2r[:, :, 0:32], rs1[:], rs2[:]), drain=False)
        b.vdo(lambda v: v.tensor_mul(rs3[:], qa, rSq[:]), drain=False)
        b.vdo(lambda v: v.tensor_mul(rs4[:], qb2, rCq[:]), drain=True)
        b.vdo(lambda v: v.tensor_add(q2r[:, :, 32:64], rs3[:], rs4[:]), drain=False)
        b.vdo(lambda v: v.tensor_mul(rs1[:], ka, rCk[:]), drain=False)
        b.vdo(lambda v: v.tensor_mul(rs2[:], kb, rSk[:]), drain=True)
        b.vdo(lambda v: v.tensor_sub(k2r[:, :, 0:32], rs1[:], rs2[:]), drain=False)
        b.vdo(lambda v: v.tensor_mul(rs3[:], ka, rSk[:]), drain=False)
        b.vdo(lambda v: v.tensor_mul(rs4[:], kb, rCk[:]), drain=True)
        b.vdo(lambda v: v.tensor_add(k2r[:, :, 32:64], rs3[:], rs4[:]), inc=True)
        b.wait("tensor", "VE", VE.n)
        for i in range(NT):
            def t6(t, i=i):
                t.transpose(psT[0:64, 0:128], q2r[:, i, :], ident[:])
                return t.transpose(psT[0:64, 128:256], k2r[:, i, :], ident[:])
            tv6 = b.tdo(t6)
            b.wait("scalar", "TE", tv6)
            def cp6(s, i=i):
                s.copy(q2f[:, 128 * i:128 * (i + 1)], psT[0:64, 0:128])
                s.copy(k2f[:, 128 * i:128 * (i + 1)], psT[0:64, 128:256])
            s6 = b.sdo(cp6, inc=True)
            scopyC[0][i] = s6
            b.wait("tensor", "SE", s6)

        def stage_rope4(p, t0):
            qa = qkvtP[p][:, t0:t0 + 4, 0:32]
            qb = qkvtP[p][:, t0:t0 + 4, 32:64]
            ka = qkvtP[p][:, t0:t0 + 4, 64:96]
            kb = qkvtP[p][:, t0:t0 + 4, 96:128]
            cq, sq = rCq[:, t0:t0 + 4, :], rSq[:, t0:t0 + 4, :]
            ck, sk = rCk[:, t0:t0 + 4, :], rSk[:, t0:t0 + 4, :]
            b.vdo(lambda v: v.tensor_mul(rtmp[:, 0:4, :], qa, cq), drain=False)
            b.vdo(lambda v: v.tensor_mul(rtmp[:, 4:8, :], qb, sq), drain=False)
            b.vdo(lambda v: v.tensor_mul(rtmp[:, 8:12, :], qa, sq), drain=False)
            b.vdo(lambda v: v.tensor_mul(rtmp[:, 12:16, :], qb, cq), drain=False)
            b.vdo(lambda v: v.tensor_mul(rtmp[:, 16:20, :], ka, ck), drain=False)
            b.vdo(lambda v: v.tensor_mul(rtmp[:, 20:24, :], kb, sk), drain=False)
            b.vdo(lambda v: v.tensor_mul(rtmp[:, 24:28, :], ka, sk), drain=False)
            b.vdo(lambda v: v.tensor_mul(rtmp[:, 28:32, :], kb, ck), drain=True)
            b.vdo(lambda v: v.tensor_sub(q2r[:, t0:t0 + 4, 0:32], rtmp[:, 0:4, :], rtmp[:, 4:8, :]), drain=False)
            b.vdo(lambda v: v.tensor_add(q2r[:, t0:t0 + 4, 32:64], rtmp[:, 8:12, :], rtmp[:, 12:16, :]), drain=False)
            b.vdo(lambda v: v.tensor_sub(k2r[:, t0:t0 + 4, 0:32], rtmp[:, 16:20, :], rtmp[:, 20:24, :]), drain=False)
            b.vdo(lambda v: v.tensor_add(k2r[:, t0:t0 + 4, 32:64], rtmp[:, 24:28, :], rtmp[:, 28:32, :]),
                  inc=True, drain=False)
            return VE.n

        injVrec = [0] * (H * NT)
        ropeB = [[0] * 4 for _ in range(H)]
        tpRec = [0] * (H * NT)
        tt6Rec = [0] * (H * NT)

        def stg(m):
            h, li = m // NT, m % NT
            tj = (li - 4) % NT
            th = (h + 1) if li >= 4 else h
            return h, li, h + 1, tj, th

        def inject6_t(m, se4):
            h, li, hs, tj, th = stg(m)
            r = m % 2
            nk = li + 1
            if hs < H:
                p = hs % 2
                if li == 0:
                    b.wait("tensor", "GD", gdW[hs])
                if m >= 2 and injVrec[m - 2]:
                    b.wait("tensor", "VE", injVrec[m - 2])
                if nk >= 5:
                    b.wait("tensor", "SE", se4[1])
                def pj(t, li=li, p=p, r=r):
                    for j in range(FC):
                        ins = t.matmul(psXp[r], hf[:, j, 128 * li:128 * (li + 1)],
                                       wqkvP[p][:, j, :], start=(j == 0), stop=(j == FC - 1))
                    return ins
                tp = b.tdo(pj)
                tpRec[m] = tp
                if li == NT - 1:
                    projC15[hs] = tp
                    if hs + 1 < H:
                        b.wait("gpsimd", "TE", projC15[hs - 1])
                        b.dma("gpsimd", wqkvP[(hs + 1) % 2][:],
                              wqkv_in[:, 192 * (hs + 1):192 * (hs + 2)].rearrange(
                                  "(j p) o -> p j o", p=128))
                        gdW[hs + 1] = GD.n
            if 1 <= th < H:
                if nk >= 9:
                    b.wait("tensor", "SE", se4[2])
                b.wait("tensor", "VE", ropeB[th][tj // 4])
                def t6i(t, tj=tj, r=r):
                    t.transpose(psTq[r], q2r[:, tj, :], ident[:])
                    return t.transpose(psTk[r], k2r[:, tj, :], ident[:])
                tt6Rec[m] = b.tdo(t6i)

        def inject6_v(m):
            h, li, hs, tj, th = stg(m)
            r = m % 2
            if hs >= H:
                return 0
            p = hs % 2
            b.wait("vector", "TE", tpRec[m])
            b.vdo(lambda v, li=li, p=p, r=r: v.tensor_copy(qkvtP[p][:, li, :], psXp[r]),
                  inc=True, drain=(li % 4 == 3))
            ve_free = VE.n
            injVrec[m] = ve_free
            if li % 4 == 3:
                ropeB[hs][li // 4] = stage_rope4(p, li - 3)
            return ve_free

        def inject6_s(m):
            h, li, hs, tj, th = stg(m)
            r = m % 2
            if not (1 <= th < H):
                return 0
            b.wait("scalar", "TE", tt6Rec[m])
            pth = th % 2
            def cp6i(s, tj=tj, pth=pth, r=r):
                s.copy(q2fP[pth][:, 128 * tj:128 * (tj + 1)], psTq[r])
                return s.copy(k2fP[pth][:, 128 * tj:128 * (tj + 1)], psTk[r])
            sc6 = b.sdo(cp6i, inc=True, drain=False)
            scopyC[th][tj] = sc6
            return sc6

        iters2 = []
        for h in range(H):
            p = h % 2
            for qi in range(NT):
                def mkout2(h=h, qi=qi):
                    return lambda sc, pso: sc.copy(oall[:, qi, 64 * h:64 * (h + 1)], pso)
                iters2.append(dict(
                    sacc=sumsB[:, qi, h:h + 1],
                    lq=q2fP[p][:, 128 * qi:128 * (qi + 1)],
                    rk=(lambda off, w, p=p: k2fP[p][:, off:off + w]),
                    nk=qi + 1,
                    vf=(lambda kt, p=p: qkvtP[p][:, kt, 128:192]),
                    out=mkout2(),
                    sw=(None if h == 0 else (lambda h=h, qi=qi: scopyC[h][qi]))))
        attn_pipe(iters2, masked=True, inject_t=inject6_t,
                  inject_v=inject6_v, inject_s=inject6_s)
        ph6_ve = VE.n
        dbg_dump("oall2", oall[:], [128, NT, D], BF16, [("VE", ph6_ve)])
        dbg_dump("qkvt", qkvt[:], [128, NT, 192], BF16, [("VE", ph6_ve)])
        dbg_dump("q2r", q2r[:], [128, NT, HD], BF16, [("VE", ph6_ve)])
        dbg_dump("k2r", k2r[:], [128, NT, HD], BF16, [("VE", ph6_ve)])
        dbg_dump("q2fd", q2f[:], [64, T], BF16, [("VE", ph6_ve)])
        dbg_dump("k2fd", k2f[:], [64, T], BF16, [("VE", ph6_ve)])

        b.wait("gpsimd", "VE", ph6_ve)
        b.dma("gpsimd", wproj_sb[:], wproj_in.rearrange("(j p) o -> p j o", p=128))
        gd_wp = GD.n
        of2 = hf
        b.vdo(lambda v: v.drain(), inc=False, drain=False)
        b.vdo(lambda v: v.reciprocal(rsB[:], sumsB[:]), inc=True, drain=True)
        for i in range(NT):
            b.vdo(lambda v, i=i: v.tensor_mul(
                e_b1[:].rearrange("p (a c) -> p a c", a=H),
                oall[:, i, :].rearrange("p (a c) -> p a c", a=H),
                rsB[:, i, :].unsqueeze(2).broadcast_to([128, H, 64])), inc=True)
            b.wait("tensor", "VE", VE.n)
            to_featmajor(i, e_b1, of2)
        b.wait("tensor", "GD", gd_wp)
        b.wait("sync", "VE", VE.n)
        for i in range(NT):
            b.dma("sync", e_xt[:], d_y[128 * i:128 * (i + 1), :])
            ldy = LD.n
            def pmm2(t, i=i):
                for j in range(FC):
                    for h2 in range(2):
                        ins = t.matmul(psX[:, 512 * h2:512 * (h2 + 1)],
                                       of2[:, j, 128 * i:128 * (i + 1)],
                                       wproj_sb[:, j, 512 * h2:512 * (h2 + 1)],
                                       start=(j == 0), stop=(j == FC - 1))
                return ins
            tm2 = b.tdo(pmm2)
            b.wait("vector", "TE", tm2)
            b.wait("vector", "LD", ldy)
            b.vdo(lambda v: v.tensor_add(e_c1[:], psX, e_xt[:]), inc=True)
            b.wait("sync", "VE", VE.n)
            b.dma("sync", d_x2[128 * i:128 * (i + 1), :], e_c1[:])
            b.wait("vector", "LD", LD.n)
            b.wait("tensor", "LD", LD.n)
        ph6f_ld = LD.n
        dbg_dump("x2", d_x2, [T, D], F32, [("LD", ph6f_ld)])

        b.wait("gpsimd", "TE", TE.n)
        w1v = w1_in.rearrange("(j p) m -> p j m", p=128)
        b.dma("gpsimd", w1sb[:, 2:8, :], w1v[:, 2:8, :])
        w2v = w2_in.rearrange("(j p) o -> p j o", p=128)
        for _wc in range(3):
            b.dma("gpsimd", w2sb[:, 8 * _wc:8 * (_wc + 1), :], w2v[:, 8 * _wc:8 * (_wc + 1), :])
        b.wait("sync", "LD", ph6f_ld)
        b.wait("sync", "VE", VE.n)
        b.dma("sync", g_v0[:], ln2_g)
        b.dma("sync", g_v1[:], ln2_b)
        b.wait("vector", "LD", LD.n)
        for i in range(NT):
            b.dma("sync", e_xt[:], d_x2[128 * i:128 * (i + 1), :])
            b.wait("vector", "LD", LD.n)
            ln_gb_tile(e_xt, e_c3, 1e-5, g_v0, g_v1, e_c1)
            b.vdo(lambda v: v.tensor_copy(e_b1[:], e_c3[:]), inc=True)
            b.wait("sync", "VE", VE.n)
            b.dma("sync", d_h2[128 * i:128 * (i + 1), :], e_b1[:])
            if i % 4 == 3:
                cb = i // 4
                b.wait("gpsimd", "LD", LD.n)
                b.coll(d_h2[512 * cb:512 * (cb + 1), :].opt(), ag_h2[cb].opt())
            b.wait("tensor", "VE", VE.n)
            se = to_featmajor(i, e_b1, h2f)
            def lmm(t, i=i):
                for j in range(FC):
                    ins = t.matmul(psL, h2f[:, j, 128 * i:128 * (i + 1)], wg_sb[:, j, :],
                                   start=(j == 0), stop=(j == FC - 1))
                return ins
            tl = b.tdo(lmm)
            b.wait("vector", "TE", tl)
            b.vdo(lambda v, i=i: v.tensor_copy(lgt[:, i, :], psL), inc=True)
            b.wait("tensor", "VE", VE.n)
            b.wait("vector", "LD", LD.n)
        b.wait("gpsimd", "TE", TE.n)
        b.dma("gpsimd", w2sb[:, 24:32, :], w2v[:, 24:32, :])
        b.dma("gpsimd", w1sb[:, 0:2, :], w1v[:, 0:2, :])
        st_w12 = GD.n
        i8bc = iota8[:].unsqueeze(1).broadcast_to([128, NT, 8])
        b.vdo(lambda v: v.reduce_max(stat0[:], lgt[:], axis=AX.X))
        b.vdo(lambda v: v.tensor_sub(t2a[:], lgt[:], stat0[:].unsqueeze(2).broadcast_to([128, NT, 8])), inc=True)
        b.wait("scalar", "VE", b.s["VE"].n)
        sg = b.sdo(lambda s: s.activation(out=t2b[:], in_=t2a[:], func=AF.Exp), inc=True)
        b.wait("vector", "SE", sg)
        b.vdo(lambda v: v.reduce_sum(stat1[:], t2b[:], axis=AX.X))
        b.vdo(lambda v: v.reciprocal(stat2[:], stat1[:]))
        b.vdo(lambda v: v.tensor_mul(gts[:], t2b[:], stat2[:].unsqueeze(2).broadcast_to([128, NT, 8])))
        b.vdo(lambda v: v.reduce_max(g1t[:], gts[:], axis=AX.X))
        b.vdo(lambda v: v.tensor_tensor(t2a[:], gts[:], g1t[:].unsqueeze(2).broadcast_to([128, NT, 8]), ALU.is_ge))
        b.vdo(lambda v: v.tensor_mul(t2b[:], t2a[:], i8bc))
        b.vdo(lambda v: v.tensor_scalar(e_s4[:], t2a[:], -8.0, 8.0, ALU.mult, ALU.add))
        b.vdo(lambda v: v.tensor_add(t2a[:], t2b[:], e_s4[:]))
        b.vdo(lambda v: v.tensor_reduce(e1t[:], t2a[:], axis=AX.X, op=ALU.min))
        b.vdo(lambda v: v.tensor_tensor(t2a[:], i8bc, e1t[:].unsqueeze(2).broadcast_to([128, NT, 8]), ALU.is_equal))
        b.vdo(lambda v: v.scalar_tensor_tensor(t2b[:], t2a[:], -1e9, gts[:], ALU.mult, ALU.add))
        b.vdo(lambda v: v.reduce_max(g2t[:], t2b[:], axis=AX.X))
        b.vdo(lambda v: v.tensor_tensor(t2a[:], t2b[:], g2t[:].unsqueeze(2).broadcast_to([128, NT, 8]), ALU.is_ge))
        b.vdo(lambda v: v.tensor_mul(t2b[:], t2a[:], i8bc))
        b.vdo(lambda v: v.tensor_scalar(e_s4[:], t2a[:], -8.0, 8.0, ALU.mult, ALU.add))
        b.vdo(lambda v: v.tensor_add(t2a[:], t2b[:], e_s4[:]))
        b.vdo(lambda v: v.tensor_reduce(e2t[:], t2a[:], axis=AX.X, op=ALU.min), inc=True)
        ve_top2 = VE.n
        dbg_dump("gates", gts[:], [128, NT, 8], F32, [("VE", ve_top2)])
        dbg_dump("e1", e1t[:], [128, NT], F32, [("VE", ve_top2)])
        dbg_dump("e2", e2t[:], [128, NT], F32, [("VE", ve_top2)])
        b.vdo(lambda v: v.tensor_copy(e_b1[:, 0:16], e1t[:]))
        b.vdo(lambda v: v.tensor_copy(e_b1[:, 16:32], e2t[:]), inc=True)
        b.wait("tensor", "VE", VE.n)
        tpy = b.tdo(lambda t: t.transpose(psT[0:32, 0:128], e_b1[:, 0:32], ident[:]))
        b.wait("scalar", "TE", tpy)
        spy = b.sdo(lambda s: s.copy(e_b2[0:32, 0:128], psT[0:32, 0:128]), inc=True)
        b.wait("sync", "SE", spy)
        b.dma("sync", d_ep1, e_b2[0:16, 0:128])
        b.dma("sync", d_ep2, e_b2[16:32, 0:128])
        ld_pay = LD.n
        b.wait("gpsimd", "LD", ld_pay)
        b.coll(d_ep1.opt(), ag_e1.opt())
        b.coll(d_ep2.opt(), ag_e2.opt())
        cc1 = CC.n


        b.wait("sync", "CC", cc1)
        e1view = ag_e1.rearrange("c a b -> (c a) b")
        e2view = ag_e2.rearrange("c a b -> (c a) b")
        b.dma("sync", e1g[:], e1view, transpose=True)
        b.dma("sync", e2g[:], e2view, transpose=True)
        b.wait("vector", "LD", LD.n)
        for e in range(E):
            b.vdo(lambda v, e=e: v.tensor_single_scalar(mgs1[:], e1g[:], float(e), ALU.is_equal))
            b.vdo(lambda v, e=e: v.tensor_single_scalar(mgs2[:], e2g[:], float(e), ALU.is_equal))
            b.vdo(lambda v, e=e: v.tensor_add(maskf[:, e, :], mgs1[:], mgs2[:]))
        b.vdo(lambda v: v.tensor_copy(maskb[:], maskf[:]), inc=True)
        b.wait("tensor", "VE", VE.n)
        mflat = maskb[:].rearrange("p e c -> p (e c)")
        def rmm(t):
            t.matmul(psPP[:, 0:512], ltri[:], mflat[:, 0:512], start=True, stop=True)
            t.matmul(psPP[:, 512:1024], ltri[:], mflat[:, 512:1024], start=True, stop=True)
            t.matmul(psCS[:, 0:512], ones[:], mflat[:, 0:512], start=True, stop=True)
            return t.matmul(psCS[:, 512:1024], ones[:], mflat[:, 512:1024], start=True, stop=True)
        trm = b.tdo(rmm)
        b.wait("vector", "TE", trm)
        b.vdo(lambda v: v.tensor_copy(ppref[:], psPP.rearrange("p (e c) -> p e c", e=E)))
        b.vdo(lambda v: v.tensor_copy(colsum[:], psCS.rearrange("p (e c) -> p e c", e=E)), inc=True)
        b.wait("tensor", "VE", VE.n)
        b.vdo(lambda v: v.memset(csh1[:, :, 0:1], 0.0))
        b.vdo(lambda v: v.tensor_copy(csh1[:, :, 1:], colsum[:, :, :-1]))
        src, dst2 = csh1, csh2
        for sh in [1, 2, 4, 8, 16, 32, 64]:
            def step(v, s=src, d=dst2, sh=sh):
                v.tensor_copy(d[:, :, :sh], s[:, :, :sh])
                v.tensor_add(d[:, :, sh:], s[:, :, sh:], s[:, :, :-sh])
            b.vdo(step)
            src, dst2 = dst2, src
        colpre = src
        b.vdo(lambda v, cp=colpre: v.tensor_add(posg[:], ppref[:], cp[:]))
        pview = posg[:].rearrange("p e (B j) -> p e j B", B=8)
        oBbc = ownB[:].unsqueeze(1).unsqueeze(2).broadcast_to([128, E, NT, 8])
        b.vdo(lambda v: v.tensor_mul(csh1[:].rearrange("p e (j B) -> p e j B", j=16), pview, oBbc))
        b.vdo(lambda v: v.reduce_sum(ownps[:], csh1[:].rearrange("p e (j B) -> p e j B", j=16), axis=AX.X))
        ownT = ownps[:].rearrange("p e j -> p j e")
        b.vdo(lambda v: v.tensor_tensor(t2a[:], i8bc, e1t[:].unsqueeze(2).broadcast_to([128, NT, 8]), ALU.is_equal))
        b.vdo(lambda v: v.tensor_mul(t2b[:], t2a[:], ownT))
        b.vdo(lambda v: v.reduce_sum(pos1[:], t2b[:], axis=AX.X))
        b.vdo(lambda v: v.tensor_tensor(t2a[:], i8bc, e2t[:].unsqueeze(2).broadcast_to([128, NT, 8]), ALU.is_equal))
        b.vdo(lambda v: v.tensor_mul(t2b[:], t2a[:], ownT))
        b.vdo(lambda v: v.reduce_sum(pos2[:], t2b[:], axis=AX.X))
        b.vdo(lambda v: v.tensor_single_scalar(kp1[:], pos1[:], float(CAP), ALU.is_lt))
        b.vdo(lambda v: v.tensor_single_scalar(kp2[:], pos2[:], float(CAP), ALU.is_lt))
        b.vdo(lambda v: v.tensor_mul(sc1[:], g1t[:], kp1[:]))
        b.vdo(lambda v: v.tensor_mul(sc2[:], g2t[:], kp2[:]))
        def emit_cidx(post, et, idxt):
            b.vdo(lambda v: v.tensor_single_scalar(stat0[:], post[:], float(CAP - 1), ALU.min))
            b.vdo(lambda v: v.tensor_single_scalar(stat1[:], stat0[:], 1024.0, ALU.is_ge))
            b.vdo(lambda v: v.tensor_single_scalar(stat2[:], stat0[:], 2048.0, ALU.is_ge))
            b.vdo(lambda v: v.tensor_add(stat3[:], stat1[:], stat2[:]))
            b.vdo(lambda v: v.tensor_single_scalar(stat1[:], stat0[:], 3072.0, ALU.is_ge))
            b.vdo(lambda v: v.tensor_add(stat2[:], stat3[:], stat1[:]))
            b.vdo(lambda v: v.scalar_tensor_tensor(stat3[:], stat2[:], 7168.0, stat0[:], ALU.mult, ALU.add))
            b.vdo(lambda v: v.scalar_tensor_tensor(stat1[:], et[:], 1024.0, stat3[:], ALU.mult, ALU.add))
            b.vdo(lambda v: v.tensor_copy(idxt[:], stat1[:]))
        emit_cidx(pos1, e1t, idx1)
        emit_cidx(pos2, e2t, idx2)
        b.vdo(lambda v: v.engine_nop(), inc=True)
        ve_own = VE.n
        dbg_dump("pos1", pos1[:], [128, NT], F32, [("VE", ve_own)])
        dbg_dump("sc1", sc1[:], [128, NT], F32, [("VE", ve_own)])
        dbg_dump("sc2", sc2[:], [128, NT], F32, [("VE", ve_own)])
        mTv = maskf[:].rearrange("p e c -> p c e")
        pTv = ppref[:].rearrange("p e c -> p c e")
        cTv = colpre[:].rearrange("p e c -> p c e")
        esbc = esel[:].unsqueeze(1).broadcast_to([128, 128, 8])
        b.vdo(lambda v: v.tensor_mul(csh1[:].rearrange("p e c -> p c e"), mTv, esbc))
        b.vdo(lambda v: v.reduce_sum(mymask[:], csh1[:].rearrange("p e c -> p c e"), axis=AX.X))
        b.vdo(lambda v: v.tensor_mul(csh1[:].rearrange("p e c -> p c e"), pTv, esbc))
        b.vdo(lambda v: v.reduce_sum(myppref[:], csh1[:].rearrange("p e c -> p c e"), axis=AX.X))
        b.vdo(lambda v: v.tensor_mul(csh1[:].rearrange("p e c -> p c e"), cTv, esbc))
        b.vdo(lambda v: v.reduce_sum(mycolpre[:], csh1[:].rearrange("p e c -> p c e"), axis=AX.X), inc=True)
        b.wait("tensor", "VE", VE.n)
        for c in range(128):
            b.vdo(lambda v, c=c: v.tensor_tensor(ohf[:], myppref[:, c:c + 1].broadcast_to([128, 128]),
                                                 iotarow[:], ALU.is_equal))
            b.vdo(lambda v, c=c: v.tensor_mul(mgs1[:], ohf[:], mymask[:, c:c + 1].broadcast_to([128, 128])))
            b.vdo(lambda v: v.tensor_copy(ohbf[:], mgs1[:]), inc=True)
            b.wait("tensor", "VE", VE.n)
            tc = b.tdo(lambda t, c=c: t.matmul(psPR[:, c:c + 1], ohbf[:], pcol[:],
                                               start=True, stop=True))
            b.wait("vector", "TE", tc)
        b.vdo(lambda v: v.tensor_copy(prow_sb[:], psPR[:, 0:128]), inc=True)
        b.wait("sync", "VE", VE.n)
        b.dma("sync", d_prow, prow_sb[:])
        ld_prow = LD.n
        b.wait("vector", "LD", ld_prow)
        cpbc = mycolpre[:].unsqueeze(1).broadcast_to([128, 32, 128])
        slbc = iotaslot[:].unsqueeze(2).broadcast_to([128, 32, 128])
        b.vdo(lambda v: v.tensor_tensor(cmpb[:], cpbc, slbc, ALU.is_le))
        b.vdo(lambda v: v.reduce_sum(colS[:], cmpb[:], axis=AX.X))
        b.vdo(lambda v: v.tensor_scalar_add(rkS[:], colS[:], -1.0))
        b.vdo(lambda v: v.tensor_copy(colS[:], rkS[:]), inc=True)
        b.vdo(lambda v: v.tensor_mul(srA[:], cmpb[:], cpbc))
        b.vdo(lambda v: v.reduce_max(cbS[:], srA[:], axis=AX.X))
        b.vdo(lambda v: v.tensor_sub(rkS[:], iotaslot[:], cbS[:]))
        b.vdo(lambda v: v.tensor_single_scalar(cbS[:], rkS[:], 127.0, ALU.min))
        b.vdo(lambda v: v.tensor_single_scalar(rkS[:], cbS[:], 0.0, ALU.max))
        b.vdo(lambda v: v.scalar_tensor_tensor(cbS[:], rkS[:], 128.0, colS[:], ALU.mult, ALU.add))
        b.vdo(lambda v: v.tensor_copy(fidx[:], cbS[:]), inc=True)
        ve_fidx = VE.n
        b.wait("gpsimd", "VE", ve_fidx)
        b.wait("gpsimd", "LD", ld_prow)
        for j in range(32):
            b.idma(prowg[:, j:j + 1], None, d_prow.rearrange("a b -> (a b)").rearrange("(n o) -> n o", o=1),
                   IndirectOffsetOnAxis(fidx[:, j:j + 1], 0))
        gd_prowg = GD.n
        b.wait("vector", "GD", gd_prowg)
        b.vdo(lambda v: v.scalar_tensor_tensor(cbS[:], colS[:], 128.0, prowg[:], ALU.mult, ALU.add))
        for k in range(1, 32):
            b.vdo(lambda v, k=k: v.tensor_single_scalar(srA[:, :, k], cbS[:], 512.0 * k, ALU.is_ge),
                  drain=(k == 31))
        b.vdo(lambda v: v.reduce_sum(rkS[:], srA[:, :, 1:32], axis=AX.X), drain=True)
        for k in range(1, 8):
            b.vdo(lambda v, k=k: v.tensor_single_scalar(srA[:, :, 32 + k], cbS[:], 2048.0 * k, ALU.is_ge),
                  drain=(k == 7))
        b.vdo(lambda v: v.reduce_sum(colS[:], srA[:, :, 33:40], axis=AX.X), drain=True)
        b.vdo(lambda v: v.scalar_tensor_tensor(srA[:, 0, 0:32], rkS[:], 3584.0, cbS[:], ALU.mult, ALU.add), drain=True)
        b.vdo(lambda v: v.scalar_tensor_tensor(srA[:, 0, 32:64], colS[:], -15872.0, srA[:, 0, 0:32], ALU.mult, ALU.add), drain=True)
        b.vdo(lambda v: v.tensor_copy(tokl[:], srA[:, 0, 32:64]), inc=True)
        ve_tokl = VE.n
        dbg_dump("tokl", tokl[:], [128, 32], I32, [("VE", ve_tokl)])

        b.wait("sync", "VE", ve_tokl)
        b.dma("sync", b2sb[:], b2_in)
        ld_b2 = LD.n
        b.wait("gpsimd", "VE", ve_tokl)
        b.wait("gpsimd", "CC", cc1)
        b.wait("tensor", "GD", st_w12)
        b.wait("vector", "LD", ld_b2)
        xgp = [xg, xg2]
        xfp = [xf, xf2]
        gX = [0] * 33
        tXf = [0] * 33
        seXf = [0] * 33
        teP1last = [0] * 33
        prevAdd = [0, 0]
        agh = ag_h2.rearrange("c e t d -> (c e t) d")
        gX[0] = b.idma(xgp[0][:], None, agh, IndirectOffsetOnAxis(tokl[:, 0:1], 0))
        for r in range(32):
            rp = r % 2
            if r + 1 < 32:
                if r >= 1:
                    b.wait("gpsimd", "TE", tXf[r - 1])
                gX[r + 1] = b.idma(xgp[(r + 1) % 2][:], None, agh,
                                   IndirectOffsetOnAxis(tokl[:, r + 1:r + 2], 0))
            b.wait("tensor", "GD", gX[r])
            if r >= 2:
                b.wait("tensor", "SE", seXf[r - 2])
            def xtp(t, rp=rp):
                for j in range(FC):
                    ins = t.transpose(psT[:, 128 * j:128 * (j + 1)],
                                      xgp[rp][:, 128 * j:128 * (j + 1)], ident[:])
                return ins
            tXf[r] = b.tdo(xtp)
            b.wait("scalar", "TE", tXf[r])
            seXf[r] = b.sdo(lambda sc, rp=rp: sc.copy(xfp[rp][:], psT[:].rearrange("p (c q) -> p c q", c=FC)),
                            inc=True, drain=False)
            b.wait("tensor", "SE", seXf[r])
            psHp = [psHm, PS[:, 2048:3072]]
            po = psOm
            hscp = [hsc, hsc2]
            hmbp = [hmb, hmb2]
            thP1 = [0] * 4
            veAdd = [0] * 4
            seG = [0] * 4

            def moe_p1(sub):
                sp = sub % 2
                if sub >= 2:
                    b.wait("tensor", "VE", veAdd[sub - 2])
                else:
                    b.wait("tensor", "VE", prevAdd[sub])
                def h1mm(t, sub=sub, rp=rp, sp=sp):
                    ins = None
                    for m in range(8):
                        mid = sub * 8 + m
                        for j in range(FC):
                            ins = t.matmul(psHp[sp][:, 128 * m:128 * (m + 1)],
                                           w1sb[:, j, 128 * mid:128 * (mid + 1)],
                                           xfp[rp][:, j, :], start=(j == 0), stop=(j == FC - 1))
                    return ins
                thP1[sub] = b.tdo(h1mm)
                b.wait("vector", "TE", thP1[sub])
                veAdd[sub] = b.vdo(lambda v, sub=sub, sp=sp: v.tensor_add(
                    hscp[sp][:].rearrange("p (a c) -> p a c", a=8),
                    psHp[sp].rearrange("p (a c) -> p a c", a=8),
                    b1sb[:, 8 * sub:8 * (sub + 1)].unsqueeze(2).broadcast_to([128, 8, 128])),
                    inc=True, drain=False)
                b.wait("scalar", "VE", veAdd[sub])
                seG[sub] = b.sdo(lambda s, sp=sp: s.activation(
                    out=hmbp[sp][:], in_=hscp[sp][:].rearrange("p (a c) -> p a c", a=8),
                    func=AF.Gelu), inc=True, drain=False)

            def moe_p2(sub):
                sp = sub % 2
                b.wait("tensor", "SE", seG[sub])
                def h2mm(t, sub=sub, sp=sp, po=po):
                    ins = None
                    for m in range(8):
                        mid = sub * 8 + m
                        for oc in range(2):
                            ins = t.matmul(po[:, 512 * oc:512 * (oc + 1)],
                                           hmbp[sp][:, m, :], w2sb[:, mid, 512 * oc:512 * (oc + 1)],
                                           start=(sub == 0 and m == 0), stop=(sub == 3 and m == 7))
                    return ins
                return b.tdo(h2mm)

            moe_p1(0)
            moe_p1(1)
            moe_p2(0)
            moe_p1(2)
            moe_p2(1)
            moe_p1(3)
            moe_p2(2)
            t2_ = moe_p2(3)
            prevAdd = [veAdd[2], veAdd[3]]
            b.wait("vector", "TE", t2_)
            teP1last[r] = TE.n
            b.vdo(lambda v, po=po: v.scalar_tensor_tensor(outt[:], po[:, 0:1024], 1.0, b2sb[:], ALU.mult, ALU.add),
                  inc=True, drain=False)
            b.wait("sync", "VE", VE.n)
            b.dma("sync", d_exp[128 * r:128 * (r + 1), :], outt[:])
            b.wait("vector", "LD", LD.n)
            if r % 8 == 7:
                cblk = r // 8
                b.wait("gpsimd", "LD", LD.n)
                b.coll(d_exp[1024 * cblk:1024 * (cblk + 1), :].opt(),
                       ag_exp[cblk].opt())
        cc2 = CC.n
        dbg_dump("expown", d_exp, [CAP, D], BF16, [("CC", cc2)])

        b.wait("gpsimd", "CC", cc2)
        agx = ag_exp.rearrange("c e t d -> (c e t) d")
        cAp = [(cA1, cA2), (cA1b, cA2b)]
        cxp = [(cx1, cx2), (cx1b, cx2b)]
        cop = [co, cob]
        gEnd = [0] * NT
        ldEnd = [0] * NT
        vUse = [0] * NT
        stC = [0] * NT
        b.idma(cAp[0][0][:], None, agx, IndirectOffsetOnAxis(idx1[:, 0:1], 0))
        gEnd[0] = b.idma(cAp[0][1][:], None, agx, IndirectOffsetOnAxis(idx2[:, 0:1], 0))
        b.dma("sync", cxp[0][0][:], d_x1[0:128, :])
        b.dma("sync", cxp[0][1][:], d_x2[0:128, :])
        ldEnd[0] = LD.n
        for i in range(NT):
            p = i % 2
            if i + 1 < NT:
                pn = (i + 1) % 2
                if i >= 1:
                    b.wait("gpsimd", "VE", vUse[i - 1])
                b.idma(cAp[pn][0][:], None, agx, IndirectOffsetOnAxis(idx1[:, i + 1:i + 2], 0))
                gEnd[i + 1] = b.idma(cAp[pn][1][:], None, agx,
                                     IndirectOffsetOnAxis(idx2[:, i + 1:i + 2], 0))
                if i >= 1:
                    b.wait("sync", "VE", vUse[i - 1])
                b.dma("sync", cxp[pn][0][:], d_x1[128 * (i + 1):128 * (i + 2), :])
                b.dma("sync", cxp[pn][1][:], d_x2[128 * (i + 1):128 * (i + 2), :])
                ldEnd[i + 1] = LD.n
            b.wait("vector", "GD", gEnd[i])
            b.wait("vector", "LD", ldEnd[i])
            b.vdo(lambda v, i=i, p=p: v.tensor_scalar_mul(ct1[:], cAp[p][0][:], sc1[:, i:i + 1]))
            b.vdo(lambda v, i=i, p=p: v.scalar_tensor_tensor(ct2[:], cAp[p][1][:], sc2[:, i:i + 1], ct1[:],
                                                             ALU.mult, ALU.add), drain=False)
            b.vdo(lambda v, p=p: v.tensor_add(ct1[:], cxp[p][0][:], cxp[p][1][:]))
            if i >= 2:
                b.wait("vector", "ST", stC[i - 2])
            vUse[i] = b.vdo(lambda v, p=p: v.tensor_add(cop[p][:], ct1[:], ct2[:]), inc=True)
            b.wait("sync", "VE", vUse[i])
            b.dma("sync", out_o[128 * i:128 * (i + 1), :], cop[p][:], sem="ST")
            stC[i] = b.s["ST"].n
        b.wait("sync", "ST", ST.n)

        blk = nc.Block()
        blk.__enter__()

        @blk.sync
        def _(e):
            for fn in b.q["sync"]:
                fn(e)

        @blk.tensor
        def _(e):
            for fn in b.q["tensor"]:
                fn(e)

        @blk.vector
        def _(e):
            for fn in b.q["vector"]:
                fn(e)

        @blk.scalar
        def _(e):
            for fn in b.q["scalar"]:
                fn(e)

        @blk.gpsimd
        def _(e):
            for fn in b.q["gpsimd"]:
                fn(e)

        blk.__exit__(None, None, None)
        nc.all_engine_barrier()
        semnums = sorted(sm.h.num for sm in b.s.values())
        srange = range(min(semnums), max(semnums) + 1)
        nc.gpsimd.dma_reset(srange)
        nc.gpsimd.sem_clear(srange)
        nc.all_engine_barrier()

    return nc



def _consts():
    c = {}
    c["ident_in"] = np.eye(128, dtype=ml_dtypes.bfloat16)
    c["ltri_in"] = (np.arange(128)[:, None] < np.arange(128)[None, :]).astype(ml_dtypes.bfloat16)
    c["ones_in"] = np.ones((128, 128), ml_dtypes.bfloat16)
    cm = np.where(np.arange(128)[None, :] <= np.arange(128)[:, None], 0.0, -1e9)
    c["cmask_in"] = cm.astype(np.float32).astype(ml_dtypes.bfloat16)
    inv = (1.0 / 10000.0 ** (np.arange(0, HD, 2, dtype=np.float64) / HD))
    tt = np.arange(T, dtype=np.float64)[:, None] * inv[None, :]
    cosf = np.cos(tt).astype(np.float32)
    sinf = np.sin(tt).astype(np.float32)
    def tokmaj(a):
        return np.ascontiguousarray(a.reshape(NT, 128, 32).transpose(1, 0, 2))
    c["rCq_in"] = (tokmaj(cosf) * 0.125).astype(ml_dtypes.bfloat16)
    c["rSq_in"] = (tokmaj(sinf) * 0.125).astype(ml_dtypes.bfloat16)
    c["rCk_in"] = tokmaj(cosf).astype(ml_dtypes.bfloat16)
    c["rSk_in"] = tokmaj(sinf).astype(ml_dtypes.bfloat16)
    c["iota8_in"] = np.broadcast_to(np.arange(8, dtype=np.float32), (128, 8)).copy()
    c["iotarow_in"] = np.broadcast_to(np.arange(128, dtype=np.float32), (128, 128)).copy()
    c["iotaslot_in"] = (np.arange(32, dtype=np.float32)[None, :] * 128
                        + np.arange(128, dtype=np.float32)[:, None]).astype(np.float32)
    c["pcol_in"] = np.arange(128, dtype=np.float32).reshape(128, 1).astype(ml_dtypes.bfloat16)
    return c


def _repl(v, n=128):
    return np.ascontiguousarray(np.broadcast_to(np.asarray(v, np.float32), (n, len(v))))


_NC_CACHE = {}


def kernel(L0_attn_x, L0_attn_ln_scale, L0_attn_ln_bias, L0_attn_Wq, L0_attn_Wkv,
           L0_attn_qn_scale, L0_attn_qn_bias, L0_attn_kn_scale, L0_attn_kn_bias,
           L0_attn_Wout, L0_ffn_ln1_g, L0_ffn_ln1_b, L0_ffn_Wqkv, L0_ffn_Wproj,
           L0_ffn_ln2_g, L0_ffn_ln2_b, L0_ffn_Wgate, L0_ffn_W1, L0_ffn_b1,
           L0_ffn_W2, L0_ffn_b2, _dbg=(), _build_only=False):
    x = np.ascontiguousarray(np.asarray(L0_attn_x, np.float32))
    consts = _consts()
    shared = dict(consts)
    wq = np.asarray(L0_attn_Wq, np.float32)
    _QA = [0, 1, 2, 3, 8, 9, 10, 11]
    _QB = [4, 5, 6, 7, 12, 13, 14, 15]
    blocks = []
    for j in range(8):
        blocks.append(wq[:, 64 * _QA[j]:64 * _QA[j] + 64])
        blocks.append(wq[:, 64 * _QB[j]:64 * _QB[j] + 64])
    shared["wq_in"] = np.ascontiguousarray(np.concatenate(blocks, axis=1))
    shared["wkv_in"] = np.asarray(L0_attn_Wkv, np.float32)
    shared["wout_in"] = np.asarray(L0_attn_Wout, np.float32)
    shared["wqkv_in"] = np.asarray(L0_ffn_Wqkv, np.float32)
    shared["wproj_in"] = np.asarray(L0_ffn_Wproj, np.float32)
    shared["wgate_in"] = np.asarray(L0_ffn_Wgate, np.float32)
    shared["lnA_g"] = _repl(L0_attn_ln_scale)
    shared["lnA_b"] = _repl(L0_attn_ln_bias)
    shared["ln1_g"] = _repl(L0_ffn_ln1_g)
    shared["ln1_b"] = _repl(L0_ffn_ln1_b)
    shared["ln2_g"] = _repl(L0_ffn_ln2_g)
    shared["ln2_b"] = _repl(L0_ffn_ln2_b)
    shared["qn_s"] = _repl(np.tile(np.asarray(L0_attn_qn_scale, np.float32), H))
    shared["qn_b"] = _repl(np.tile(np.asarray(L0_attn_qn_bias, np.float32), H))
    shared["kn_s"] = _repl(np.tile(np.asarray(L0_attn_kn_scale, np.float32), G4))
    shared["kn_b"] = _repl(np.tile(np.asarray(L0_attn_kn_bias, np.float32), G4))

    W1 = np.asarray(L0_ffn_W1, np.float32)
    W2 = np.asarray(L0_ffn_W2, np.float32)
    b1 = np.asarray(L0_ffn_b1, np.float32)
    b2 = np.asarray(L0_ffn_b2, np.float32)

    in_maps = []
    for ccore in range(8):
        m = dict(shared)
        m["x_in"] = np.ascontiguousarray(x[ccore])
        m["w1_in"] = np.ascontiguousarray(W1[ccore])
        m["w2_in"] = np.ascontiguousarray(W2[ccore])
        m["b1_in"] = np.ascontiguousarray(b1[ccore].reshape(32, 128).T)
        m["b2_in"] = _repl(b2[ccore])
        oh = np.zeros((128, 8), np.float32)
        oh[:, ccore] = 1.0
        m["esel_in"] = oh.copy()
        m["ownB_in"] = oh.copy()
        in_maps.append(m)

    key = tuple(sorted(_dbg))
    if key not in _NC_CACHE:
        _NC_CACHE[key] = build_nc(dbg=_dbg)
    nc = _NC_CACHE[key]
    if _build_only:
        return nc, in_maps
    res = run_bass_kernel_spmd(nc, in_maps, core_ids=list(range(8)))
    out = np.stack([np.asarray(res.results[i]["out_o"], np.float32) for i in range(8)])
    if _dbg:
        return out, res
    return out
